# version 37
# baseline (speedup 1.0000x reference)
import numpy as np
import ml_dtypes
import contextlib
import concourse.bass as bass
import concourse.mybir as mybir
from concourse.bass_utils import run_bass_kernel_spmd

F32 = mybir.dt.float32
BF16 = mybir.dt.bfloat16
FP8 = mybir.dt.float8e4
AF = mybir.ActivationFunctionType
ALU = mybir.AluOpType

H = 1024
E = 2048
SD = 128
KC = H // 128
EC = E // 128

ENGS = ("sync", "scalar", "vector", "tensor", "gpsimd")


class Plan:

    def __init__(self):
        self.progs = {e: [] for e in ENGS}
        self.sem_cnt = {n: 0 for n in (
            "lds", "lx0", "lx1", "w0", "w1", "o0", "o1",
            "p0", "p1", "q0", "q1", "ga0", "ga1", "pe", "v", "s", "g")}
        self.waited = {e: {} for e in ENGS}

    def val(self, name):
        return self.sem_cnt[name]

    def op(self, eng, fn, waits=(), inc=None, inc_amt=1):
        real = []
        for sn, v in waits:
            if v <= 0 or self.waited[eng].get(sn, 0) >= v:
                continue
            self.waited[eng][sn] = v
            real.append((sn, v))
        post = None
        if inc is not None:
            self.sem_cnt[inc] += inc_amt
            post = self.sem_cnt[inc]
        self.progs[eng].append((real, fn, inc, inc_amt))
        return post


def build_program(seqs, S):
    TC = S // 128
    NT = min(512, S)
    NNT = S // NT
    BT = min(512, S)
    NBT = S // BT

    nc = bass.Bass("TRN2", target_bir_lowering=False)

    x_d = nc.dram_tensor("x", [seqs, S, H], F32, kind="ExternalInput")
    wuv_d = nc.dram_tensor("wuv", [H, 2 * E + SD], F32, kind="ExternalInput")
    wout_d = nc.dram_tensor("wout", [E, H], F32, kind="ExternalInput")
    wffnT_d = nc.dram_tensor("wffnT", [H, H], F32, kind="ExternalInput")
    ident_d = nc.dram_tensor("ident", [128, 128], BF16, kind="ExternalInput")
    bu_d = nc.dram_tensor("bu_cols", [128, EC], F32, kind="ExternalInput")
    bbase_d = nc.dram_tensor("bbase_col", [128, 1], F32, kind="ExternalInput")
    gb_d = nc.dram_tensor("gb_cols", [128, 4], F32, kind="ExternalInput")
    bv_d = nc.dram_tensor("bv_row", [1, E], F32, kind="ExternalInput")
    bout_d = nc.dram_tensor("bout_row", [1, H], F32, kind="ExternalInput")
    bffn_d = nc.dram_tensor("bffn_row", [1, H], F32, kind="ExternalInput")
    sin_d = nc.dram_tensor("s_in", [1, 1], F32, kind="ExternalInput")
    wsc_d = nc.dram_tensor("w_scale", [1, 1], F32, kind="ExternalInput")
    out_d = nc.dram_tensor("out", [seqs, S, H], F32, kind="ExternalOutput")

    NSC = 15
    sc_d = nc.dram_tensor("wscratch", [NSC, 128, 4096], BF16)

    P = Plan()

    with contextlib.ExitStack() as stack:
        En = stack.enter_context

        wbuf = [En(nc.sbuf_tensor(f"wbuf{i}", [128, 4096], BF16)) for i in range(2)]
        wbase = En(nc.sbuf_tensor("wbase", [128, KC * 128], BF16))
        ones_bf = En(nc.sbuf_tensor("ones_bf", [1, 128], BF16))
        bv_bf = En(nc.sbuf_tensor("bv_bf", [1, E], BF16))
        bffn_bf = En(nc.sbuf_tensor("bffn_bf", [1, H], BF16))
        ones32 = En(nc.sbuf_tensor("ones32", [1, 128], F32))
        bout_bc = En(nc.sbuf_tensor("bout_bc", [128, H], F32))
        ident = En(nc.sbuf_tensor("identt", [128, 128], BF16))
        bu_cols = En(nc.sbuf_tensor("bu_cols_t", [128, EC], F32))
        bbase_col = En(nc.sbuf_tensor("bbase_col_t", [128, 1], F32))
        gb_cols = En(nc.sbuf_tensor("gb_cols_t", [128, 4], F32))
        scomb_bc = En(nc.sbuf_tensor("scomb_bc", [128, 1], F32))
        inv2s_bc = En(nc.sbuf_tensor("inv2s_bc", [128, 1], F32))
        s_t = En(nc.sbuf_tensor("s_t", [1, 1], F32))
        w_t = En(nc.sbuf_tensor("w_t", [1, 1], F32))
        s2_t = En(nc.sbuf_tensor("s2_t", [1, 1], F32))
        sc2_t = En(nc.sbuf_tensor("sc2_t", [1, 1], F32))
        isc_t = En(nc.sbuf_tensor("isc_t", [1, 1], F32))
        i2s_t = En(nc.sbuf_tensor("i2s_t", [1, 1], F32))
        mask_t = En(nc.sbuf_tensor("mask_t", [1, 1], F32))
        c_t = En(nc.sbuf_tensor("c_t", [1, 1], F32))
        rci_t = En(nc.sbuf_tensor("rci_t", [1, 1], F32))
        cb_t = En(nc.sbuf_tensor("cb_t", [1, 1], F32))
        ncb_t = En(nc.sbuf_tensor("ncb_t", [1, 1], F32))
        ci2s_t = En(nc.sbuf_tensor("ci2s_t", [1, 1], F32))
        cb_bc = En(nc.sbuf_tensor("cb_bc", [128, 1], F32))
        ncb_bc = En(nc.sbuf_tensor("ncb_bc", [128, 1], F32))
        stbn = En(nc.sbuf_tensor("stbn", [128, TC * 12], F32))
        stmv = En(nc.sbuf_tensor("stmv", [128, TC * 2], F32))
        ve_all = En(nc.sbuf_tensor("ve_all", [128, TC], F32))
        rc_all = En(nc.sbuf_tensor("rc_all", [128, TC], F32))
        rs_all = En(nc.sbuf_tensor("rs_all", [128, TC], F32))
        nb_all = En(nc.sbuf_tensor("nb_all", [128, TC], F32))
        rs2s_all = En(nc.sbuf_tensor("rs2s_all", [128, TC], F32))
        nb2_all = En(nc.sbuf_tensor("nb2_all", [128, TC], F32))
        stbn2 = En(nc.sbuf_tensor("stbn2", [128, TC * 12], F32))
        stmv2 = En(nc.sbuf_tensor("stmv2", [128, TC * 2], F32))
        rs2_all = En(nc.sbuf_tensor("rs2_all", [128, TC], F32))
        ve2_all = En(nc.sbuf_tensor("ve2_all", [128, TC], F32))
        rc2_all = En(nc.sbuf_tensor("rc2_all", [128, TC], F32))
        lnT0 = En(nc.sbuf_tensor("lnT0", [128, H], BF16))
        wstage = [En(nc.sbuf_tensor(f"wstage{i}", [128, 512], F32)) for i in range(2)]
        wc8s = En(nc.sbuf_tensor("wc8s", [128, 512], BF16))

        A0 = (nc.sbuf_base + 127) & ~127
        sz_sq = TC * NT * 2
        sz_A = max(KC * S * 2, 2 * sz_sq, 32768)
        sz_QK = max(2 * S * 2, 2 * H * 4)
        sz_V = max(EC * S * 2, TC * H * 4)
        sz_U = max(EC * S * 2, 32768 + max(KC * S * 2, 24576))
        oA, oQK = A0, A0 + sz_A
        oV, oU = A0 + sz_A + sz_QK, A0 + sz_A + sz_QK + sz_V
        oYF = oU + 32768
        assert oU + sz_U <= nc.sbuf_top, (oU + sz_U, nc.sbuf_top)

        AA = nc.alloc_sbuf_tensor_at
        ln = AA("ln", [128, KC * S], BF16, offset=oA)
        sq_b = [AA(f"sq{i}", [128, TC * NT], BF16, offset=oA + i * sz_sq)
                for i in range(2)]
        wout_big = AA("woutb", [128, 16384], BF16, offset=oA)
        qk = AA("qk", [128, 2 * S], BF16, offset=oQK)
        x_t = [AA(f"xt{i}", [128, H], F32, offset=oQK + i * 4096) for i in range(2)]
        v_t = AA("vt", [128, EC * S], BF16, offset=oV)
        x1_b = AA("x1b", [128, TC * H], F32, offset=oV)
        base_t = AA("baset", [128, S], F32, offset=oU)
        u_t = AA("ut", [128, EC * S], BF16, offset=oU)
        th_s = [AA(f"th{i}", [128, H], F32, offset=oU + i * 4096) for i in range(2)]
        tc_s = [AA(f"tcs{i}", [128, H], F32, offset=oU + 8192 + i * 4096) for i in range(2)]
        y8_s = [AA(f"y8{i}", [128, H], FP8, offset=oU + 16384 + i * 1024) for i in range(2)]
        y16_s = [AA(f"y16{i}", [128, H], BF16, offset=oU + 18432 + i * 2048) for i in range(2)]
        out_t = [AA(f"ot{i}", [128, H], F32, offset=oU + 22528 + i * 4096) for i in range(2)]
        lnT1 = AA("lnT1", [128, H], BF16, offset=oU + 30720)
        y_f = AA("yf", [128, KC * S], BF16, offset=oYF)
        wsl32 = [AA(f"wsl32_{i}", [128, H], F32, offset=oYF + i * 4096)
                 for i in range(2)]
        wsl16 = [AA(f"wsl16_{i}", [128, H], BF16, offset=oYF + 8192 + i * 2048)
                 for i in range(2)]
        rowtmp = AA("rowtmp", [1, E], F32, offset=oA)
        rowtmp2 = AA("rowtmp2", [1, H], F32, offset=oA + 8192)
        rowtmp3 = AA("rowtmp3", [1, H], F32, offset=oA + 12288)

        lnT_scr = [lnT0, lnT1]
        t_relu = [lnT0[:, 512:512 + NT], lnT0[:, 0:NT]]


        def x1_tile(t):
            return x1_b[:, t * H:(t + 1) * H]

        pw = [En(nc.psum_tensor(f"pw{i}", [128, 1024], F32)) for i in range(2)]
        pn = [En(nc.psum_tensor(f"pn{i}", [128, 512], F32)) for i in range(2)]
        p_mm = [pw[0][:, 0:512], pw[0][:, 512:1024]]
        PB = [pw[0][:, 0:512], pw[0][:, 512:1024],
              pw[1][:, 0:512], pw[1][:, 512:1024], pn[0][:], pn[1][:]]
        rot = [0]
        p_tp = En(nc.psum_tensor("ptp", [128, 1024], BF16))
        p_tp2 = En(nc.psum_tensor("ptp2", [128, 1024], BF16))

        sems = {n: En(nc.semaphore(n)) for n in (
            "lds", "lx0", "lx1", "w0", "w1", "o0", "o1",
            "p0", "p1", "q0", "q1", "ga0", "ga1", "pe", "v", "s", "g")}

        Z = ("pe", 0)
        st = {
            "pb": [Z] * 6, "ptp": Z, "ptp2": Z, "lnTh": [Z, Z],
            "xt": [Z, Z], "lnT": [Z, Z], "trelu": [Z, Z],
            "wbuf": [Z, Z], "wbase": Z, "outt": [Z, Z],
            "th": [Z, Z], "tcs": [Z, Z], "y8": [Z, Z], "y16": [Z, Z],
        }

        def dma(eng, dst, src, sem, waits=(), self_wait=True):
            w = list(waits)
            if self_wait:
                w.append((sem, P.val(sem)))
            return P.op(eng, lambda e, d=dst, s=src: e.dma_start(out=d, in_=s),
                        waits=w, inc=sem, inc_amt=16)

        dma("sync", ident[:], ident_d.ap(), "lds", self_wait=False)
        dma("sync", bu_cols[:], bu_d.ap(), "lds", self_wait=False)
        dma("sync", bbase_col[:], bbase_d.ap(), "lds", self_wait=False)
        dma("sync", gb_cols[:], gb_d.ap(), "lds", self_wait=False)
        dma("sync", rowtmp[:], bv_d.ap(), "lds", self_wait=False)
        dma("sync", rowtmp2[:], bout_d.ap(), "lds", self_wait=False)
        dma("sync", rowtmp3[:], bffn_d.ap(), "lds", self_wait=False)
        dma("sync", s_t[:], sin_d.ap(), "lds", self_wait=False)
        dma("sync", w_t[:], wsc_d.ap(), "lds", self_wait=False)
        ld_small = P.val("lds")

        P.op("vector", lambda e: e.memset(ones_bf[:], 1.0))
        P.op("vector", lambda e: e.memset(ones32[:], 1.0))
        P.op("vector", lambda e: e.tensor_copy(bv_bf[:], rowtmp[:]),
             waits=[("lds", ld_small)])
        P.op("vector", lambda e: e.tensor_scalar_mul(
            gb_cols[:, 0:2], gb_cols[:, 0:2], 1.0 / S))
        P.op("vector", lambda e: e.tensor_tensor(s2_t[:], s_t[:], w_t[:], op=ALU.mult))
        P.op("vector", lambda e: e.tensor_tensor(sc2_t[:], s_t[:], s_t[:], op=ALU.mult))
        P.op("vector", lambda e: e.reciprocal(i2s_t[:], s_t[:]))
        P.op("vector", lambda e: e.drain())
        vv0 = P.op("vector", lambda e: e.tensor_scalar_add(sc2_t[:], sc2_t[:], -0.0225),
                   inc="v")
        sv0 = P.op("scalar", lambda e: e.activation(mask_t[:], sc2_t[:], AF.Sign),
                   waits=[("v", vv0)], inc="s")
        P.op("vector", lambda e: e.tensor_scalar(c_t[:], mask_t[:], 0.25, 0.75,
                                                 op0=ALU.mult, op1=ALU.add),
             waits=[("s", sv0)])
        P.op("vector", lambda e: e.tensor_scalar(rci_t[:], mask_t[:], -0.5, 1.5,
                                                 op0=ALU.mult, op1=ALU.add))
        P.op("vector", lambda e: e.drain())
        P.op("vector", lambda e: e.tensor_tensor(sc2_t[:], s2_t[:], rci_t[:], op=ALU.mult))
        P.op("vector", lambda e: e.tensor_tensor(ci2s_t[:], i2s_t[:], c_t[:], op=ALU.mult))
        P.op("vector", lambda e: e.tensor_scalar_mul(cb_t[:], c_t[:], 448.0))
        P.op("vector", lambda e: e.drain())
        P.op("vector", lambda e: e.reciprocal(isc_t[:], sc2_t[:]))
        P.op("vector", lambda e: e.tensor_scalar_mul(ncb_t[:], cb_t[:], -1.0))
        P.op("vector", lambda e: e.drain())
        P.op("vector", lambda e: e.tensor_scalar_mul(rowtmp3[:], rowtmp3[:], isc_t[:]))
        P.op("vector", lambda e: e.drain())
        v_pre = P.op("vector", lambda e: e.tensor_copy(bffn_bf[:], rowtmp3[:]), inc="v")
        P.op("tensor", lambda e: e.matmul(
            PB[0][:, 0:512], ones32[:], rowtmp2[0:1, 0:512], start=True, stop=True),
            waits=[("v", v_pre), ("lds", ld_small)])
        P.op("tensor", lambda e: e.matmul(
            PB[1][:, 0:512], ones32[:], rowtmp2[0:1, 512:1024], start=True, stop=True))
        P.op("tensor", lambda e: e.matmul(
            pn[0][:, 0:1], ones32[:], sc2_t[:], start=True, stop=True))
        P.op("tensor", lambda e: e.matmul(
            pn[0][:, 1:2], ones32[:], ci2s_t[:], start=True, stop=True))
        P.op("tensor", lambda e: e.matmul(
            pn[0][:, 2:3], ones32[:], cb_t[:], start=True, stop=True))
        pe_bc = P.op("tensor", lambda e: e.matmul(
            pn[0][:, 3:4], ones32[:], ncb_t[:], start=True, stop=True), inc="pe")
        P.op("vector", lambda e: e.tensor_copy(bout_bc[:, 0:512], PB[0][:, 0:512]),
             waits=[("pe", pe_bc)])
        P.op("vector", lambda e: e.tensor_copy(bout_bc[:, 512:1024], PB[1][:, 0:512]))
        P.op("vector", lambda e: e.tensor_copy(scomb_bc[:], pn[0][:, 0:1]))
        P.op("vector", lambda e: e.tensor_copy(inv2s_bc[:], pn[0][:, 1:2]))
        P.op("vector", lambda e: e.tensor_copy(cb_bc[:], pn[0][:, 2:3]))
        v_bc = P.op("vector", lambda e: e.tensor_copy(ncb_bc[:], pn[0][:, 3:4]),
                    inc="v")
        st["pb"][0] = st["pb"][1] = ("v", v_bc)
        st["pb"][4] = ("v", v_bc)

        pre_v = P.val("v")
        scr_ready = {}
        scr_q = [0]

        def scr_store(sc_idx, src_ap, cast_v):
            qi = scr_q[0] % 2
            scr_q[0] += 1
            v = P.op("scalar", lambda e, d=sc_idx, src=src_ap:
                     e.dma_start(out=sc_d.ap()[d][:, 0:src.shape[-1]], in_=src),
                     waits=[("v", cast_v), ("q%d" % qi, P.val("q%d" % qi))],
                     inc="q%d" % qi, inc_amt=16)
            scr_ready[sc_idx] = [("q%d" % qi, v)]

        rA_free = [("v", pre_v), ("pe", pe_bc)]
        rV_free = [("v", pre_v)]
        rU_free = [("v", pre_v)]
        qk_gate = [("v", pre_v)]

        QV, QU, QB, QF = list(range(4)), list(range(4, 8)), 8, [13, 14]

        stg_free = [("v", 0), ("v", 0)]

        def load_panel(s, dst_fn, sc_idx, cols, f32_srcs, wsem, free_w, gate):
            fw = list(free_w) if isinstance(free_w, list) else [free_w]
            if s > 0:
                g = scr_ready[sc_idx] if s == 1 else []
                v = dma("sync", dst_fn(0, cols), sc_d.ap()[sc_idx][:, 0:cols],
                        wsem, waits=fw + g)
                return [(wsem, v)]
            cv = 0
            for i, (src, col, w) in enumerate(f32_srcs):
                sl = i % 2
                ldv = dma("sync", wstage[sl][:, 0:w], src, "w%d" % sl,
                          waits=[stg_free[sl]] + (fw if i == 0 else []))
                cv = P.op("vector", lambda e, sl=sl, col=col, w=w:
                          e.tensor_copy(dst_fn(col, w), wstage[sl][:, 0:w]),
                          waits=[("w%d" % sl, ldv)] + (fw if i == 0 else []),
                          inc="v")
                stg_free[sl] = ("v", cv)
            scr_store(sc_idx, dst_fn(0, cols), cv)
            return [("v", cv)]

        wpre_q = [0]
        wc8_free = [("v", 0)]

        def emit_wout_precast_piece():
            i, k = wpre_q[0] // KC, wpre_q[0] % KC
            wpre_q[0] += 1
            hh, rr = i // 2, i % 2
            row = 9 + i
            sl = wpre_q[0] % 2
            g = "ga%d" % sl
            ldv = P.op("scalar", lambda e, rr=rr, k=k, hh=hh, sl=sl: e.dma_start(
                out=wstage[sl][:],
                in_=wout_d.ap()[(rr * 8 + k) * 128:(rr * 8 + k + 1) * 128,
                                hh * 512:(hh + 1) * 512]),
                waits=[stg_free[sl], (g, P.val(g))], inc=g, inc_amt=16)
            cv = P.op("vector", lambda e, sl=sl: e.tensor_copy(
                wc8s[:], wstage[sl][:]),
                waits=[(g, ldv), wc8_free[0]], inc="v")
            stg_free[sl] = ("v", cv)
            stv = P.op("scalar", lambda e, row=row, k=k, sl=sl: e.dma_start(
                out=sc_d.ap()[row][:, k * 512:(k + 1) * 512],
                in_=wc8s[:]),
                waits=[("v", cv), ("q%d" % sl, P.val("q%d" % sl))],
                inc="q%d" % sl, inc_amt=16)
            wc8_free[0] = ("q%d" % sl, stv)
            if k == KC - 1:
                scr_ready[row] = [("q0", P.val("q0")),
                                  ("q1", P.val("q1"))]
            state["wpre_done"] = [("q0", P.val("q0")), ("q1", P.val("q1"))]

        state = {
            "p3_pe": 0, "p4_pe": 0, "ffn_pe": 0,
            "ln_evac": [("s", 0)] * TC,
            "qk_ready": 0, "v_ready": 0, "u_ready": 0,
            "sq_ready": [0, 0],
        }

        tp3d = p_tp[:].rearrange("p (c t) -> p c t", c=KC)
        tp2_3d = p_tp2[:].rearrange("p (c t) -> p c t", c=KC)

        state["stats1"] = 0

        def emit_P1A_tile(s, t, xgate=()):
            b = t % 2
            ldv = dma("sync", x_t[b][:], x_d.ap()[s, t * 128:(t + 1) * 128, :],
                      "lx%d" % b, waits=[st["xt"][b], ("pe", state["p3_pe"])] +
                      list(xgate))
            P.op("vector", lambda e, t=t, b=b: e.bn_stats(
                stbn[:, t * 12:t * 12 + 6], x_t[b][:, 0:512]),
                waits=[("lx%d" % b, ldv)])
            vv = P.op("vector", lambda e, t=t, b=b: e.bn_stats(
                stbn[:, t * 12 + 6:t * 12 + 12], x_t[b][:, 512:1024]), inc="v")
            st["xt"][b] = ("v", vv)

        pend = {"q_rc": 0, "tail_rc": None}

        def emit_P1A_tail(s, defer=False):
            P.op("vector", lambda e: e.drain())
            for t in range(TC):
                P.op("vector", lambda e, t=t: e.bn_aggr(
                    stmv[:, 2 * t:2 * t + 2], stbn[:, 12 * t:12 * t + 12]))
            P.op("vector", lambda e: e.drain())
            P.op("vector", lambda e: e.tensor_scalar_add(
                ve_all[:], stmv[:].rearrange("p (t c) -> p t c", c=2)[:, :, 1], 1e-5))
            P.op("vector", lambda e: e.drain())
            vv = P.op("vector", lambda e: e.reciprocal(rc_all[:], ve_all[:]), inc="v")
            if defer:
                pend["tail_rc"] = vv
                return
            sv = P.op("scalar", lambda e: e.activation(rs_all[:], rc_all[:], AF.Sqrt),
                      waits=[("v", vv)], inc="s")
            state["stats1"] = P.op("vector", lambda e: e.scalar_tensor_tensor(
                nb_all[:], stmv[:].rearrange("p (t c) -> p t c", c=2)[:, :, 0],
                -1.0, rs_all[:], op0=ALU.mult, op1=ALU.mult),
                waits=[("s", sv)], inc="v")

        ln3d = ln[:].rearrange("p (c t) -> p c t", c=KC)

        p1b_sv = {}

        def emit_P1B_norm(s, t):
            b = t % 2
            hw = [st["lnTh"][0], st["lnTh"][1]] if t % 2 == 0 else []
            if s == 0:
                sv = P.op("scalar", lambda e, t=t: e.activation(
                    lnT_scr[t % 2][:], x1_tile(t), AF.Identity,
                    scale=rs_all[:, t:t + 1], bias=nb_all[:, t:t + 1]),
                    waits=[("v", state["stats1"]), st["lnT"][t % 2]] + hw,
                    inc="s")
                p1b_sv[t] = sv
                return
            ldv = dma("sync", x_t[b][:], x_d.ap()[s, t * 128:(t + 1) * 128, :],
                      "lx%d" % b, waits=[st["xt"][b]])
            sv = P.op("scalar", lambda e, t=t, b=b: e.activation(
                lnT_scr[t % 2][:], x_t[b][:], AF.Identity,
                scale=rs_all[:, t:t + 1], bias=nb_all[:, t:t + 1]),
                waits=[("lx%d" % b, ldv), ("v", state["stats1"]),
                       st["lnT"][t % 2]] + hw,
                inc="s")
            st["xt"][b] = ("s", sv)
            p1b_sv[t] = sv

        def emit_P1B_tp(s, t, rA, pre=False):
            sv = p1b_sv[t]
            odd = pre and t % 2 == 1
            bank, b3d, key = (p_tp, tp3d, "ptp") if odd else \
                             (p_tp2, tp2_3d, "ptp2")
            pv = None
            for hc in range(KC):
                w = [("s", sv), st[key]] if hc == 0 else []
                pv = P.op("tensor", lambda e, hc=hc, t=t, bank=bank: e.transpose(
                    bank[:, hc * 128:(hc + 1) * 128],
                    lnT_scr[t % 2][:, hc * 128:(hc + 1) * 128], ident[:]),
                    waits=w, inc="pe" if hc == KC - 1 else None)
            st["lnT"][t % 2] = ("pe", pv)
            if t % 2 == 0:
                st["lnTh"][0] = st["lnTh"][1] = ("pe", pv)
            if pre:
                ev = P.op("vector", lambda e, t=t, b3d=b3d: e.tensor_copy(
                    ln3d[:, :, t * 128:(t + 1) * 128], b3d),
                    waits=[("pe", pv)] + (rA if t == 0 else []), inc="v")
                st[key] = ("v", ev)
                state["ln_evac"][t] = ("v", ev)
            else:
                ev = P.op("scalar", lambda e, t=t: e.activation(
                    ln3d[:, :, t * 128:(t + 1) * 128], tp2_3d, AF.Copy),
                    waits=[("pe", pv)] + (rA if t == 0 else []), inc="s")
                st[key] = ("s", ev)
                state["ln_evac"][t] = ("s", ev)

        def emit_P1B_tile(s, t, rA, pre=False):
            emit_P1B_norm(s, t)
            emit_P1B_tp(s, t, rA, pre=pre)

        stats2_q = [0] * ((TC + 3) // 4)

        def emit_ln2_quad(q):
            a, bq = 4 * q, min(4 * q + 4, TC)
            P.op("vector", lambda e: e.drain())
            for t in range(a, bq):
                P.op("vector", lambda e, t=t: e.bn_aggr(
                    stmv2[:, 2 * t:2 * t + 2], stbn2[:, 12 * t:12 * t + 12]))
            P.op("vector", lambda e: e.drain())
            P.op("vector", lambda e, a=a, bq=bq: e.tensor_scalar_add(
                ve2_all[:, a:bq],
                stmv2[:, 2 * a:2 * bq].rearrange("p (t c) -> p t c", c=2)[:, :, 1],
                1e-5))
            P.op("vector", lambda e: e.drain())
            pend["q_rc"] = P.op("vector", lambda e, a=a, bq=bq: e.reciprocal(
                rc2_all[:, a:bq], ve2_all[:, a:bq]), inc="v")

        def emit_stats_finish(s):
            sv = P.op("scalar", lambda e: e.activation(
                rs2_all[:, 0:TC], rc2_all[:, 0:TC], AF.Sqrt),
                waits=[("v", pend["q_rc"])], inc="s")
            sv2 = None
            if pend["tail_rc"] is not None:
                sv2 = P.op("scalar", lambda e: e.activation(
                    rs_all[:], rc_all[:], AF.Sqrt),
                    waits=[("v", pend["tail_rc"])], inc="s")
            P.op("vector", lambda e: e.tensor_scalar(
                rs2s_all[:, 0:TC], rs2_all[:, 0:TC], inv2s_bc[:], 0.0,
                op0=ALU.mult, op1=ALU.add), waits=[("s", sv)])
            P.op("vector", lambda e: e.drain())
            qv = P.op("vector", lambda e: e.scalar_tensor_tensor(
                nb2_all[:, 0:TC],
                stmv2[:, 0:2 * TC].rearrange("p (t c) -> p t c", c=2)[:, :, 0],
                -1.0, rs2s_all[:, 0:TC], op0=ALU.mult, op1=ALU.mult), inc="v")
            for q in range(len(stats2_q)):
                stats2_q[q] = qv
            if sv2 is not None:
                state["stats1"] = P.op("vector", lambda e: e.scalar_tensor_tensor(
                    nb_all[:], stmv[:].rearrange("p (t c) -> p t c", c=2)[:, :, 0],
                    -1.0, rs_all[:], op0=ALU.mult, op1=ALU.mult),
                    waits=[("s", sv2)], inc="v")
                pend["tail_rc"] = None

        def emit_P1(s, rA, xgate=()):
            if s == 0:
                for t in range(TC):
                    dma("sync", x1_tile(t),
                        x_d.ap()[0, t * 128:(t + 1) * 128, :],
                        "lx0", self_wait=False)
                allx = P.val("lx0")
                for t in range(TC):
                    P.op("vector", lambda e, t=t: e.bn_stats(
                        stbn[:, t * 12:t * 12 + 6], x1_tile(t)[:, 0:512]),
                        waits=[("lx0", allx)])
                    P.op("vector", lambda e, t=t: e.bn_stats(
                        stbn[:, t * 12 + 6:t * 12 + 12],
                        x1_tile(t)[:, 512:1024]))
            else:
                for t in range(TC):
                    emit_P1A_tile(s, t, xgate)
            emit_P1A_tail(s)
            for t in range(TC):
                emit_P1B_tile(s, t, rA, pre=True)

        def make_scores(j, sqi):
            ops = []
            for mc in range(TC):
                def emit(mc=mc, j=j, sqi=sqi):
                    b = mc % 2
                    pv = P.op("tensor", lambda e, mc=mc, j=j, b=b: e.matmul(
                        p_mm[b][:, 0:NT],
                        qk[:, S + mc * 128:S + (mc + 1) * 128],
                        qk[:, j * NT:(j + 1) * NT],
                        start=True, stop=True),
                        waits=[("s", state["qk_ready"]), st["pb"][b]],
                        inc="pe")
                    sv = P.op("scalar", lambda e, mc=mc, b=b: e.activation(
                        t_relu[b], p_mm[b][:, 0:NT], AF.Relu),
                        waits=[("pe", pv), st["trelu"][b], st["lnTh"][b]],
                        inc="s")
                    st["pb"][b] = ("s", sv)
                    vv = P.op("vector", lambda e, mc=mc, sqi=sqi, b=b: e.tensor_tensor(
                        sq_b[sqi][:, mc * NT:(mc + 1) * NT], t_relu[b],
                        t_relu[b], op=ALU.mult),
                        waits=[("s", sv)], inc="v")
                    st["trelu"][b] = ("v", vv)
                    st["lnTh"][b] = ("v", vv)
                    state["sq_ready"][sqi] = vv
                ops.append(emit)
            return ops

        pending_scores = []

        def drip_scores(n=1):
            for _ in range(n):
                if pending_scores:
                    pending_scores.pop(0)()

        emit_P1(0, rA_free, xgate=[("v", pre_v)])
        for s in range(seqs):
            ln_evac = list(state["ln_evac"])
            sq0_dead_pe = 0

            wvbw = state.pop("wvbw_next", None)
            if wvbw is None:
                wvbw = load_panel(
                    s, lambda col, w: wbase[:, col:col + w], QB, KC * 128,
                    [(wuv_d.ap()[k * 128:(k + 1) * 128, 2 * E:2 * E + SD],
                      k * 128, SD) for k in range(KC)],
                    "w0", st["wbase"], [])
            for bt in range(NBT):
                pi = rot[0] % 4
                rot[0] += 1
                tb0 = (bt * BT) // 128
                for k in range(KC):
                    w = wvbw + list(ln_evac[tb0:tb0 + BT // 128]) + \
                        [st["pb"][pi]] if k == 0 else []
                    P.op("tensor", lambda e, k=k, bt=bt, pi=pi: e.matmul(
                        PB[pi][:, 0:BT],
                        wbase[:, k * 128:(k + 1) * 128],
                        ln[:, k * S + bt * BT:k * S + (bt + 1) * BT],
                        start=(k == 0), stop=(k == KC - 1)),
                        waits=w, inc="pe" if k == KC - 1 else None)
                pv = P.val("pe")
                sv = P.op("scalar", lambda e, bt=bt, pi=pi: e.activation(
                    base_t[:, bt * BT:(bt + 1) * BT], PB[pi][:, 0:BT], AF.Silu,
                    bias=bbase_col[:]),
                    waits=[("pe", pv)] + (rU_free if bt == 0 else []), inc="s")
                st["pb"][pi] = ("s", sv)
            base_pe = P.val("pe")
            st["wbase"] = [("pe", base_pe)] + (scr_ready[QB] if s == 0 else [])
            P.op("scalar", lambda e: e.drain())
            P.op("scalar", lambda e: e.activation(
                qk[:, 0:S], base_t[:], AF.Identity,
                scale=gb_cols[:, 0:1], bias=gb_cols[:, 1:2]),
                waits=qk_gate)
            state["qk_ready"] = P.op("scalar", lambda e: e.activation(
                qk[:, S:2 * S], base_t[:], AF.Identity,
                scale=gb_cols[:, 2:3], bias=gb_cols[:, 3:4]), inc="s")
            P.op("scalar", lambda e: e.drain())

            for qi in range(4):
                wsl = qi % 2
                wvw = load_panel(
                    s, lambda col, w, wsl=wsl: wbuf[wsl][:, col:col + w],
                    QV[qi], 4096,
                    [(wuv_d.ap()[k * 128:(k + 1) * 128,
                                 E + qi * 512:E + (qi + 1) * 512], k * 512, 512)
                     for k in range(KC)],
                    "w%d" % wsl, st["wbuf"][wsl], [])
                last_pe = None
                for t in range(TC):
                    pi = rot[0] % 4
                    rot[0] += 1
                    for k in range(KC):
                        w = wvw + [ln_evac[t], st["pb"][pi]] \
                            if k == 0 else []
                        P.op("tensor", lambda e, k=k, t=t, pi=pi, wsl=wsl: e.matmul(
                            PB[pi][:, 0:512],
                            ln[:, k * S + t * 128:k * S + (t + 1) * 128],
                            wbuf[wsl][:, k * 512:(k + 1) * 512],
                            start=(k == 0), stop=False), waits=w)
                    pv = P.op("tensor", lambda e, q=qi, pi=pi: e.matmul(
                        PB[pi][:, 0:512], ones_bf[:],
                        bv_bf[0:1, q * 512:(q + 1) * 512],
                        start=False, stop=True), inc="pe")
                    last_pe = pv
                    sv = P.op("scalar", lambda e, t=t, q=qi, pi=pi: e.activation(
                        v_t[:, t * E + q * 512:t * E + (q + 1) * 512],
                        PB[pi][:, 0:512], AF.Silu),
                        waits=[("pe", pv)] + (rV_free if (t == 0 and qi == 0) else []),
                        inc="s")
                    st["pb"][pi] = ("s", sv)
                st["wbuf"][wsl] = [("pe", last_pe)] + \
                    (scr_ready[QV[qi]] if s == 0 else [])
            state["v_ready"] = P.val("s")

            mm_count = 0
            for qi in range(4):
                wsl = qi % 2
                wvw = load_panel(
                    s, lambda col, w, wsl=wsl: wbuf[wsl][:, col:col + w],
                    QU[qi], 4096,
                    [(wuv_d.ap()[k * 128:(k + 1) * 128,
                                 qi * 512:(qi + 1) * 512], k * 512, 512)
                     for k in range(KC)],
                    "w%d" % wsl, st["wbuf"][wsl], [])
                last_pe = None
                for es in range(4):
                    ec = qi * 4 + es
                    for ntj in range(NNT):
                        pi = rot[0] % 4
                        rot[0] += 1
                        t_lo, t_hi = (ntj * NT) // 128, ((ntj + 1) * NT - 1) // 128
                        for k in range(KC):
                            w = wvw + list(ln_evac[t_lo:t_hi + 1]) + \
                                [st["pb"][pi]] if k == 0 else []
                            P.op("tensor", lambda e, k=k, es=es, ntj=ntj, pi=pi, wsl=wsl: e.matmul(
                                PB[pi][:, 0:NT],
                                wbuf[wsl][:, k * 512 + es * 128:k * 512 + (es + 1) * 128],
                                ln[:, k * S + ntj * NT:k * S + (ntj + 1) * NT],
                                start=(k == 0), stop=(k == KC - 1)),
                                waits=w, inc="pe" if k == KC - 1 else None)
                        pv = P.val("pe")
                        last_pe = pv
                        sv = P.op("scalar", lambda e, ec=ec, ntj=ntj, pi=pi: e.activation(
                            u_t[:, ec * S + ntj * NT:ec * S + (ntj + 1) * NT],
                            PB[pi][:, 0:NT], AF.Silu, bias=bu_cols[:, ec:ec + 1]),
                            waits=[("pe", pv)], inc="s")
                        st["pb"][pi] = ("s", sv)
                        mm_count += 1
                st["wbuf"][wsl] = [("pe", last_pe)] + \
                    (scr_ready[QU[qi]] if s == 0 else [])
            state["u_ready"] = P.val("s")

            wf0 = load_panel(
                s, lambda col, w: wbuf[0][:, col:col + w], QF[0], 4096,
                [(wffnT_d.ap()[k * 128:(k + 1) * 128, h * 512:(h + 1) * 512],
                  (k % 4) * 1024 + h * 512, 512)
                 for k in range(4) for h in range(2)],
                "w0", st["wbuf"][0], [])
            wf1 = load_panel(
                s, lambda col, w: wbuf[1][:, col:col + w], QF[1], 4096,
                [(wffnT_d.ap()[k * 128:(k + 1) * 128, h * 512:(h + 1) * 512],
                  ((k - 4) % 4) * 1024 + h * 512, 512)
                 for k in range(4, 8) for h in range(2)],
                "w1", st["wbuf"][1], [])

            p1a_q = list(range(TC)) if s + 1 < seqs else []
            p1a_tail_due = s + 1 < seqs

            def drip_p1a(gate=()):
                if p1a_q:
                    emit_P1A_tile(s + 1, p1a_q.pop(0), xgate=gate)

            h_done_v = 0
            for j in range(NNT):
                sqi = j % 2
                own_scores = make_scores(j, sqi) if j == 0 else []
                sq_mc_v = [0] * TC
                if j == 0:
                    for k in range(min(3, TC)):
                        own_scores[k]()
                        sq_mc_v[k] = state["sq_ready"][sqi]
                if j + 1 < NNT:
                    pending_scores = make_scores(j + 1, (j + 1) % 2)
                else:
                    pending_scores = []
                for eq in range(4):
                    es_pe = [0] * 4
                    if j == 0 and eq == 0:
                        for mc in range(TC):
                            for idx in (2 * mc + 3, 2 * mc + 4):
                                if idx < TC:
                                    own_scores[idx]()
                                    sq_mc_v[idx] = state["sq_ready"][sqi]
                            for es in range(4):
                                w = [("v", sq_mc_v[mc])]
                                if mc == 0:
                                    w += [("s", state["v_ready"]), st["pb"][2 + es]]
                                P.op("tensor", lambda e, eq=eq, mc=mc, es=es, sqi=sqi: e.matmul(
                                    PB[2 + es][:, 0:NT],
                                    v_t[:, mc * E + eq * 512 + es * 128:
                                        mc * E + eq * 512 + (es + 1) * 128],
                                    sq_b[sqi][:, mc * NT:(mc + 1) * NT],
                                    start=(mc == 0), stop=(mc == TC - 1)),
                                    waits=w,
                                    inc="pe" if mc == TC - 1 else None)
                                if mc == TC - 1:
                                    es_pe[es] = P.val("pe")
                    else:
                        for es in range(4):
                            for mc in range(TC):
                                w = []
                                if mc == 0:
                                    w = [("v", state["sq_ready"][sqi]),
                                         ("s", state["v_ready"]), st["pb"][2 + es]]
                                P.op("tensor", lambda e, eq=eq, mc=mc, es=es, sqi=sqi: e.matmul(
                                    PB[2 + es][:, 0:NT],
                                    v_t[:, mc * E + eq * 512 + es * 128:
                                        mc * E + eq * 512 + (es + 1) * 128],
                                    sq_b[sqi][:, mc * NT:(mc + 1) * NT],
                                    start=(mc == 0), stop=(mc == TC - 1)),
                                    waits=w,
                                    inc="pe" if mc == TC - 1 else None)
                            es_pe[es] = P.val("pe")
                            drip_scores(1)
                            ec = eq * 4 + es
                            vv = P.op("vector", lambda e, ec=ec, es=es, j=j: e.tensor_tensor(
                                u_t[:, ec * S + j * NT:ec * S + (j + 1) * NT],
                                PB[2 + es][:, 0:NT],
                                u_t[:, ec * S + j * NT:ec * S + (j + 1) * NT],
                                op=ALU.mult),
                                waits=[("pe", es_pe[es]), ("s", state["u_ready"])],
                                inc="v")
                            st["pb"][2 + es] = ("v", vv)
                            h_done_v = vv
                            if s == 0 and wpre_q[0] < 4 * KC:
                                emit_wout_precast_piece()
                    if j == 0 and eq == 0:
                        for es in range(4):
                            ec = eq * 4 + es
                            vv = P.op("vector", lambda e, ec=ec, es=es, j=j: e.tensor_tensor(
                                u_t[:, ec * S + j * NT:ec * S + (j + 1) * NT],
                                PB[2 + es][:, 0:NT],
                                u_t[:, ec * S + j * NT:ec * S + (j + 1) * NT],
                                op=ALU.mult),
                                waits=[("pe", es_pe[es]), ("s", state["u_ready"])], inc="v")
                            st["pb"][2 + es] = ("v", vv)
                            h_done_v = vv
                    if j == NNT - 1 and j > 0:
                        drip_p1a(gate=[("pe", P.val("pe"))])
                drip_scores(TC)
                if sqi == 0:
                    sq0_dead_pe = P.val("pe")
            h_ready = h_done_v
            if s == 0:
                while wpre_q[0] < 4 * KC:
                    emit_wout_precast_piece()
            p3_pe = P.val("pe")
            state["p3_pe"] = p3_pe

            wo_w = [None, None]
            for i in range(4):
                gate = sq0_dead_pe if i < 2 and NNT >= 2 else p3_pe
                hh, rr = i // 2, i % 2
                load_panel(
                    max(s, 1), lambda col, w, i=i: wout_big[:, i * 4096 + col:
                                                            i * 4096 + col + w],
                    9 + i, 4096, [],
                    "w%d" % (i % 2), ("pe", gate), [])
                if i == 1:
                    wo_w[0] = [("w0", P.val("w0")), ("w1", P.val("w1")),
                               ("v", P.val("v"))]
            wo_w[1] = [("w0", P.val("w0")), ("w1", P.val("w1")),
                       ("v", P.val("v"))]
            x1_pre = {}

            def emit_x1pre(t):
                b = t % 2
                ldv = dma("sync", x_t[b][:], x_d.ap()[s, t * 128:(t + 1) * 128, :],
                          "lx%d" % b, waits=[st["xt"][b], ("pe", p3_pe)])
                vv = P.op("vector", lambda e, t=t, b=b: e.scalar_tensor_tensor(
                    x1_tile(t), x_t[b][:], 1.0, bout_bc[:],
                    op0=ALU.mult, op1=ALU.add),
                    waits=[("lx%d" % b, ldv), ("pe", p3_pe)], inc="v")
                st["xt"][b] = ("v", vv)
                x1_pre[t] = ("v", vv)

            for t in range(min(2, TC)):
                emit_x1pre(t)
            for t in range(TC):
                wi = t % 2
                x1ld = x1_pre[t]
                for hh in range(2):
                    for ec in range(EC):
                        w = []
                        if ec == 0:
                            w = wo_w[hh] + [("v", h_ready),
                                            st["pb"][2 * wi], st["pb"][2 * wi + 1]]
                        P.op("tensor", lambda e, ec=ec, t=t, hh=hh, wi=wi: e.matmul(
                            pw[wi][:, hh * 512:(hh + 1) * 512],
                            u_t[:, ec * S + t * 128:ec * S + (t + 1) * 128],
                            wout_big[:, hh * 8192 + (ec // 8) * 4096 +
                                     (ec % 8) * 512:
                                     hh * 8192 + (ec // 8) * 4096 + (ec % 8) * 512 + 512],
                            start=(ec == 0), stop=(ec == EC - 1)),
                            waits=w, inc="pe" if (ec == EC - 1 and hh == 1) else None)
                pv = P.val("pe")
                vv = P.op("vector", lambda e, t=t, wi=wi: e.tensor_tensor(
                    x1_tile(t), pw[wi][:, 0:1024], x1_tile(t), op=ALU.add),
                    waits=[("pe", pv), x1ld], inc="v")
                st["pb"][2 * wi] = st["pb"][2 * wi + 1] = ("v", vv)
                x1_done = P.val("v")
                P.op("vector", lambda e, t=t: e.bn_stats(
                    stbn2[:, t * 12:t * 12 + 6], x1_tile(t)[:, 0:512]),
                    waits=[("v", x1_done)])
                P.op("vector", lambda e, t=t: e.bn_stats(
                    stbn2[:, t * 12 + 6:t * 12 + 12], x1_tile(t)[:, 512:1024]))
                if t + 2 < TC:
                    emit_x1pre(t + 2)
                drip_p1a()
                if p1a_tail_due and not p1a_q:
                    emit_P1A_tail(s + 1, defer=True)
                    p1a_tail_due = False
                if t % 4 == 0 and t > 0:
                    emit_ln2_quad(t // 4 - 1)
                if t == TC - 1:
                    emit_ln2_quad(t // 4)
            rA_next = [("pe", P.val("pe"))]
            p4_pe = P.val("pe")
            state["p4_pe"] = p4_pe
            emit_stats_finish(s)

            yf3d = y_f[:].rearrange("p (c t) -> p c t", c=KC)
            y_evac = [0] * TC
            y_evac2 = [0] * TC
            th_v = [0] * TC
            q_done = [0] * TC

            def emit_th(t):
                b = t % 2
                th_v[t] = P.op("scalar", lambda e, t=t, b=b: e.activation(
                    th_s[b][:], x1_tile(t), AF.Identity,
                    scale=rs2s_all[:, t:t + 1], bias=nb2_all[:, t:t + 1]),
                    waits=[("v", stats2_q[t // 4]), st["th"][b], ("pe", p4_pe)],
                    inc="s")

            def emit_quant(t):
                b = t % 2
                vv = P.op("vector", lambda e, b=b: e.tensor_scalar(
                    tc_s[b][:], th_s[b][:], cb_bc[:], ncb_bc[:],
                    op0=ALU.min, op1=ALU.max),
                    waits=[("s", th_v[t]), st["tcs"][b]], inc="v")
                st["th"][b] = ("v", vv)
                sv = P.op("scalar", lambda e, b=b: e.activation(
                    y8_s[b][:], tc_s[b][:], AF.Copy),
                    waits=[("v", vv), st["y8"][b]], inc="s")
                st["tcs"][b] = ("s", sv)
                vv = P.op("vector", lambda e, b=b: e.tensor_copy(
                    y16_s[b][:], y8_s[b][:]),
                    waits=[("s", sv), st["y16"][b]], inc="v")
                st["y8"][b] = ("v", vv)
                q_done[t] = vv

            pending_out = None

            def emit_out_stt(t, wi, ob, pv):
                vv = P.op("vector", lambda e, t=t, wi=wi, ob=ob: e.scalar_tensor_tensor(
                    out_t[ob][:], pw[wi][:, 0:1024], scomb_bc[:],
                    x1_tile(t), op0=ALU.mult, op1=ALU.add),
                    waits=[("pe", pv), st["outt"][ob]], inc="v")
                st["pb"][2 * wi] = st["pb"][2 * wi + 1] = ("v", vv)
                stv = dma("sync", out_d.ap()[s, t * 128:(t + 1) * 128, :], out_t[ob][:],
                          "o%d" % ob, waits=[("v", P.val("v"))])
                st["outt"][ob] = ("o%d" % ob, stv)

            emit_th(0)
            if TC > 1:
                emit_th(1)
            emit_quant(0)
            if s + 1 < seqs:
                emit_P1B_norm(s + 1, 0)
            for t in range(TC):
                b = t % 2
                pv = None
                for hc in range(KC):
                    w = [("v", q_done[t]), st["ptp"]] if hc == 0 else []
                    pv = P.op("tensor", lambda e, hc=hc, b=b: e.transpose(
                        p_tp[:, hc * 128:(hc + 1) * 128],
                        y16_s[b][:, hc * 128:(hc + 1) * 128], ident[:]),
                        waits=w, inc="pe" if hc == KC - 1 else None)
                st["y16"][b] = ("pe", pv)
                vv = P.op("vector", lambda e, t=t: e.tensor_copy(
                    yf3d[:, 0:KC // 2, t * 128:(t + 1) * 128],
                    tp3d[:, 0:KC // 2, :]),
                    waits=[("pe", pv)] +
                          (state.get("wpre_done", [])
                           if s == 0 and t == 0 else []), inc="v")
                y_evac[t] = vv
                vv = P.op("vector", lambda e, t=t: e.tensor_copy(
                    yf3d[:, KC // 2:KC, t * 128:(t + 1) * 128],
                    tp3d[:, KC // 2:KC, :]), inc="v")
                st["ptp"] = ("v", vv)
                y_evac2[t] = vv
                if t + 1 < TC:
                    emit_quant(t + 1)
                if t == min(8, TC - 1) and s + 1 < seqs:
                    state["wvbw_next"] = load_panel(
                        s + 1, lambda col, w: wbase[:, col:col + w],
                        QB, KC * 128, [], "w0", st["wbase"], [])
                if s + 1 < seqs:
                    emit_P1B_tp(s + 1, t, rA_next)
                    if t + 1 < TC:
                        emit_P1B_norm(s + 1, t + 1)
                if t + 2 < TC:
                    emit_th(t + 2)
                if pending_out is not None:
                    emit_out_stt(*pending_out)
                ob = t % 2
                wi = t % 2
                for ot in range(2):
                    for k in range(KC):
                        w = []
                        if k == 0:
                            w = wf0 + wf1 + [("v", y_evac[t]),
                                 st["pb"][2 * wi], st["pb"][2 * wi + 1]]
                        elif k == KC // 2:
                            w = [("v", y_evac2[t])]
                        P.op("tensor", lambda e, k=k, t=t, ot=ot, wi=wi: e.matmul(
                            pw[wi][:, ot * 512:(ot + 1) * 512],
                            y_f[:, k * S + t * 128:k * S + (t + 1) * 128],
                            wbuf[k // 4][:, (k % 4) * 1024 + ot * 512:
                                         (k % 4) * 1024 + (ot + 1) * 512],
                            start=(k == 0), stop=False), waits=w)
                    P.op("tensor", lambda e, ot=ot, wi=wi: e.matmul(
                        pw[wi][:, ot * 512:(ot + 1) * 512], ones_bf[:],
                        bffn_bf[0:1, ot * 512:(ot + 1) * 512],
                        start=False, stop=True),
                        inc="pe" if ot == 1 else None)
                pending_out = (t, wi, ob, P.val("pe"))
            emit_out_stt(*pending_out)
            pending_out = None
            ffn_pe = P.val("pe")
            st["wbuf"][0] = [("pe", ffn_pe)] + (scr_ready[QF[0]] if s == 0 else [])
            st["wbuf"][1] = [("pe", ffn_pe)] + (scr_ready[QF[1]] if s == 0 else [])

            rV_free = [("v", P.val("v")), ("s", P.val("s"))]
            rU_free = [("o0", P.val("o0")), ("o1", P.val("o1")), ("v", P.val("v")),
                       ("pe", ffn_pe), ("s", P.val("s"))]
            qk_gate = []
            P.op("vector", lambda e: e.drain())
            P.op("scalar", lambda e: e.drain())

        P.op("sync", lambda e: e.nop(),
             waits=[("o0", P.val("o0")), ("o1", P.val("o1"))])

        with nc.Block() as block:
            def body(eng_name):
                def f(eng):
                    for waits, fn, inc, inc_amt in P.progs[eng_name]:
                        for sn, v in waits:
                            eng.wait_ge(sems[sn], v)
                        r = fn(eng)
                        if inc is not None:
                            r.then_inc(sems[inc], inc_amt)
                return f
            block.sync(body("sync"))
            block.scalar(body("scalar"))
            block.vector(body("vector"))
            block.tensor(body("tensor"))
            block.gpsimd(body("gpsimd"))

    return nc


def prep_aux(L0_attn_W_uv, L0_attn_b_uv, L0_attn_gamma, L0_attn_beta,
             L0_attn_W_out, L0_attn_b_out, L0_ffn_weight_fp8, L0_ffn_bias,
             L0_ffn_input_scale, L0_ffn_weight_scale):
    gb = np.stack([
        np.asarray(L0_attn_gamma)[0], np.asarray(L0_attn_beta)[0],
        np.asarray(L0_attn_gamma)[1], np.asarray(L0_attn_beta)[1],
    ], axis=1).astype(np.float32)
    b_uv = np.asarray(L0_attn_b_uv, dtype=np.float32)
    return {
        "wuv": np.ascontiguousarray(np.asarray(L0_attn_W_uv, dtype=np.float32)),
        "wout": np.ascontiguousarray(np.asarray(L0_attn_W_out, dtype=np.float32)),
        "wffnT": np.ascontiguousarray(np.asarray(L0_ffn_weight_fp8, dtype=np.float32).T),
        "ident": np.eye(128, dtype=ml_dtypes.bfloat16),
        "bu_cols": np.ascontiguousarray(b_uv[0:E].reshape(EC, 128).T),
        "bbase_col": np.ascontiguousarray(b_uv[2 * E:2 * E + SD].reshape(128, 1)),
        "gb_cols": np.ascontiguousarray(gb),
        "bv_row": np.ascontiguousarray(b_uv[E:2 * E].reshape(1, E)),
        "bout_row": np.ascontiguousarray(
            np.asarray(L0_attn_b_out, dtype=np.float32).reshape(1, H)),
        "bffn_row": np.ascontiguousarray(
            np.asarray(L0_ffn_bias, dtype=np.float32).reshape(1, H)),
        "s_in": np.asarray(L0_ffn_input_scale, dtype=np.float32).reshape(1, 1),
        "w_scale": np.asarray(L0_ffn_weight_scale, dtype=np.float32).reshape(1, 1),
    }


def kernel(L0_attn_inputs, L0_attn_W_uv, L0_attn_b_uv, L0_attn_gamma,
           L0_attn_beta, L0_attn_W_out, L0_attn_b_out, L0_ffn_weight_fp8,
           L0_ffn_bias, L0_ffn_input_scale, L0_ffn_weight_scale):
    n_cores = 8
    x = np.ascontiguousarray(np.asarray(L0_attn_inputs, dtype=np.float32))
    B, S, _ = x.shape
    seqs = B // n_cores
    aux = prep_aux(L0_attn_W_uv, L0_attn_b_uv, L0_attn_gamma, L0_attn_beta,
                   L0_attn_W_out, L0_attn_b_out, L0_ffn_weight_fp8,
                   L0_ffn_bias, L0_ffn_input_scale, L0_ffn_weight_scale)
    nc = build_program(seqs, S)
    in_maps = []
    for c in range(n_cores):
        m = {"x": x[c * seqs:(c + 1) * seqs]}
        m.update(aux)
        in_maps.append(m)
    res = run_bass_kernel_spmd(nc, in_maps, core_ids=list(range(n_cores)))
    return np.concatenate([r["out"] for r in res.results], axis=0)


# version 38
# speedup vs baseline: 1.0021x; 1.0021x over previous
import numpy as np
import ml_dtypes
import contextlib
import concourse.bass as bass
import concourse.mybir as mybir
from concourse.bass_utils import run_bass_kernel_spmd

F32 = mybir.dt.float32
BF16 = mybir.dt.bfloat16
FP8 = mybir.dt.float8e4
AF = mybir.ActivationFunctionType
ALU = mybir.AluOpType

H = 1024
E = 2048
SD = 128
KC = H // 128
EC = E // 128

ENGS = ("sync", "scalar", "vector", "tensor", "gpsimd")


class Plan:

    def __init__(self):
        self.progs = {e: [] for e in ENGS}
        self.sem_cnt = {n: 0 for n in (
            "lds", "lx0", "lx1", "w0", "w1", "o0", "o1",
            "p0", "p1", "q0", "q1", "ga0", "ga1", "pe", "v", "s", "g")}
        self.waited = {e: {} for e in ENGS}

    def val(self, name):
        return self.sem_cnt[name]

    def op(self, eng, fn, waits=(), inc=None, inc_amt=1):
        real = []
        for sn, v in waits:
            if v <= 0 or self.waited[eng].get(sn, 0) >= v:
                continue
            self.waited[eng][sn] = v
            real.append((sn, v))
        post = None
        if inc is not None:
            self.sem_cnt[inc] += inc_amt
            post = self.sem_cnt[inc]
        self.progs[eng].append((real, fn, inc, inc_amt))
        return post


def build_program(seqs, S):
    TC = S // 128
    NT = min(512, S)
    NNT = S // NT
    BT = min(512, S)
    NBT = S // BT

    nc = bass.Bass("TRN2", target_bir_lowering=False)

    x_d = nc.dram_tensor("x", [seqs, S, H], F32, kind="ExternalInput")
    wuv_d = nc.dram_tensor("wuv", [H, 2 * E + SD], F32, kind="ExternalInput")
    wout_d = nc.dram_tensor("wout", [E, H], F32, kind="ExternalInput")
    wffnT_d = nc.dram_tensor("wffnT", [H, H], F32, kind="ExternalInput")
    ident_d = nc.dram_tensor("ident", [128, 128], BF16, kind="ExternalInput")
    bu_d = nc.dram_tensor("bu_cols", [128, EC], F32, kind="ExternalInput")
    bbase_d = nc.dram_tensor("bbase_col", [128, 1], F32, kind="ExternalInput")
    gb_d = nc.dram_tensor("gb_cols", [128, 4], F32, kind="ExternalInput")
    bv_d = nc.dram_tensor("bv_row", [1, E], F32, kind="ExternalInput")
    bout_d = nc.dram_tensor("bout_row", [1, H], F32, kind="ExternalInput")
    bffn_d = nc.dram_tensor("bffn_row", [1, H], F32, kind="ExternalInput")
    sin_d = nc.dram_tensor("s_in", [1, 1], F32, kind="ExternalInput")
    wsc_d = nc.dram_tensor("w_scale", [1, 1], F32, kind="ExternalInput")
    out_d = nc.dram_tensor("out", [seqs, S, H], F32, kind="ExternalOutput")

    NSC = 15
    sc_d = nc.dram_tensor("wscratch", [NSC, 128, 4096], BF16)

    P = Plan()

    with contextlib.ExitStack() as stack:
        En = stack.enter_context

        wbuf = [En(nc.sbuf_tensor(f"wbuf{i}", [128, 4096], BF16)) for i in range(2)]
        wbase = En(nc.sbuf_tensor("wbase", [128, KC * 128], BF16))
        ones_bf = En(nc.sbuf_tensor("ones_bf", [1, 128], BF16))
        bv_bf = En(nc.sbuf_tensor("bv_bf", [1, E], BF16))
        bffn_bf = En(nc.sbuf_tensor("bffn_bf", [1, H], BF16))
        ones32 = En(nc.sbuf_tensor("ones32", [1, 128], F32))
        bout_bc = En(nc.sbuf_tensor("bout_bc", [128, H], F32))
        ident = En(nc.sbuf_tensor("identt", [128, 128], BF16))
        bu_cols = En(nc.sbuf_tensor("bu_cols_t", [128, EC], F32))
        bbase_col = En(nc.sbuf_tensor("bbase_col_t", [128, 1], F32))
        gb_cols = En(nc.sbuf_tensor("gb_cols_t", [128, 4], F32))
        scomb_bc = En(nc.sbuf_tensor("scomb_bc", [128, 1], F32))
        inv2s_bc = En(nc.sbuf_tensor("inv2s_bc", [128, 1], F32))
        s_t = En(nc.sbuf_tensor("s_t", [1, 1], F32))
        w_t = En(nc.sbuf_tensor("w_t", [1, 1], F32))
        s2_t = En(nc.sbuf_tensor("s2_t", [1, 1], F32))
        sc2_t = En(nc.sbuf_tensor("sc2_t", [1, 1], F32))
        isc_t = En(nc.sbuf_tensor("isc_t", [1, 1], F32))
        i2s_t = En(nc.sbuf_tensor("i2s_t", [1, 1], F32))
        mask_t = En(nc.sbuf_tensor("mask_t", [1, 1], F32))
        c_t = En(nc.sbuf_tensor("c_t", [1, 1], F32))
        rci_t = En(nc.sbuf_tensor("rci_t", [1, 1], F32))
        cb_t = En(nc.sbuf_tensor("cb_t", [1, 1], F32))
        ncb_t = En(nc.sbuf_tensor("ncb_t", [1, 1], F32))
        ci2s_t = En(nc.sbuf_tensor("ci2s_t", [1, 1], F32))
        cb_bc = En(nc.sbuf_tensor("cb_bc", [128, 1], F32))
        ncb_bc = En(nc.sbuf_tensor("ncb_bc", [128, 1], F32))
        stbn = En(nc.sbuf_tensor("stbn", [128, TC * 12], F32))
        stmv = En(nc.sbuf_tensor("stmv", [128, TC * 2], F32))
        ve_all = En(nc.sbuf_tensor("ve_all", [128, TC], F32))
        rc_all = En(nc.sbuf_tensor("rc_all", [128, TC], F32))
        rs_all = En(nc.sbuf_tensor("rs_all", [128, TC], F32))
        nb_all = En(nc.sbuf_tensor("nb_all", [128, TC], F32))
        rs2s_all = En(nc.sbuf_tensor("rs2s_all", [128, TC], F32))
        nb2_all = En(nc.sbuf_tensor("nb2_all", [128, TC], F32))
        stbn2 = En(nc.sbuf_tensor("stbn2", [128, TC * 12], F32))
        stmv2 = En(nc.sbuf_tensor("stmv2", [128, TC * 2], F32))
        rs2_all = En(nc.sbuf_tensor("rs2_all", [128, TC], F32))
        ve2_all = En(nc.sbuf_tensor("ve2_all", [128, TC], F32))
        rc2_all = En(nc.sbuf_tensor("rc2_all", [128, TC], F32))
        lnT0 = En(nc.sbuf_tensor("lnT0", [128, H], BF16))
        wstage = [En(nc.sbuf_tensor(f"wstage{i}", [128, 512], F32)) for i in range(2)]
        wc8s = En(nc.sbuf_tensor("wc8s", [128, 512], BF16))

        A0 = (nc.sbuf_base + 127) & ~127
        sz_sq = TC * NT * 2
        sz_A = max(KC * S * 2, 2 * sz_sq, 32768)
        sz_QK = max(2 * S * 2, 2 * H * 4)
        sz_V = max(EC * S * 2, TC * H * 4)
        sz_U = max(EC * S * 2, 32768 + max(KC * S * 2, 24576))
        oA, oQK = A0, A0 + sz_A
        oV, oU = A0 + sz_A + sz_QK, A0 + sz_A + sz_QK + sz_V
        oYF = oU + 32768
        assert oU + sz_U <= nc.sbuf_top, (oU + sz_U, nc.sbuf_top)

        AA = nc.alloc_sbuf_tensor_at
        ln = AA("ln", [128, KC * S], BF16, offset=oA)
        sq_b = [AA(f"sq{i}", [128, TC * NT], BF16, offset=oA + i * sz_sq)
                for i in range(2)]
        wout_big = AA("woutb", [128, 16384], BF16, offset=oA)
        qk = AA("qk", [128, 2 * S], BF16, offset=oQK)
        x_t = [AA(f"xt{i}", [128, H], F32, offset=oQK + i * 4096) for i in range(2)]
        v_t = AA("vt", [128, EC * S], BF16, offset=oV)
        x1_b = AA("x1b", [128, TC * H], F32, offset=oV)
        base_t = AA("baset", [128, S], F32, offset=oU)
        u_t = AA("ut", [128, EC * S], BF16, offset=oU)
        th_s = [AA(f"th{i}", [128, H], F32, offset=oU + i * 4096) for i in range(2)]
        tc_s = [AA(f"tcs{i}", [128, H], F32, offset=oU + 8192 + i * 4096) for i in range(2)]
        y8_s = [AA(f"y8{i}", [128, H], FP8, offset=oU + 16384 + i * 1024) for i in range(2)]
        y16_s = [AA(f"y16{i}", [128, H], BF16, offset=oU + 18432 + i * 2048) for i in range(2)]
        out_t = [AA(f"ot{i}", [128, H], F32, offset=oU + 22528 + i * 4096) for i in range(2)]
        lnT1 = AA("lnT1", [128, H], BF16, offset=oU + 30720)
        y_f = AA("yf", [128, KC * S], BF16, offset=oYF)
        wsl32 = [AA(f"wsl32_{i}", [128, H], F32, offset=oYF + i * 4096)
                 for i in range(2)]
        wsl16 = [AA(f"wsl16_{i}", [128, H], BF16, offset=oYF + 8192 + i * 2048)
                 for i in range(2)]
        rowtmp = AA("rowtmp", [1, E], F32, offset=oA)
        rowtmp2 = AA("rowtmp2", [1, H], F32, offset=oA + 8192)
        rowtmp3 = AA("rowtmp3", [1, H], F32, offset=oA + 12288)

        lnT_scr = [lnT0, lnT1]
        t_relu = [lnT0[:, 512:512 + NT], lnT0[:, 0:NT]]


        def x1_tile(t):
            return x1_b[:, t * H:(t + 1) * H]

        pw = [En(nc.psum_tensor(f"pw{i}", [128, 1024], F32)) for i in range(2)]
        pn = [En(nc.psum_tensor(f"pn{i}", [128, 512], F32)) for i in range(2)]
        p_mm = [pw[0][:, 0:512], pw[0][:, 512:1024]]
        PB = [pw[0][:, 0:512], pw[0][:, 512:1024],
              pw[1][:, 0:512], pw[1][:, 512:1024], pn[0][:], pn[1][:]]
        rot = [0]
        p_tp = En(nc.psum_tensor("ptp", [128, 1024], BF16))
        p_tp2 = En(nc.psum_tensor("ptp2", [128, 1024], BF16))

        sems = {n: En(nc.semaphore(n)) for n in (
            "lds", "lx0", "lx1", "w0", "w1", "o0", "o1",
            "p0", "p1", "q0", "q1", "ga0", "ga1", "pe", "v", "s", "g")}

        Z = ("pe", 0)
        st = {
            "pb": [Z] * 6, "ptp": Z, "ptp2": Z, "lnTh": [Z, Z],
            "xt": [Z, Z], "lnT": [Z, Z], "trelu": [Z, Z],
            "wbuf": [Z, Z], "wbase": Z, "outt": [Z, Z],
            "th": [Z, Z], "tcs": [Z, Z], "y8": [Z, Z], "y16": [Z, Z],
        }

        def dma(eng, dst, src, sem, waits=(), self_wait=True):
            w = list(waits)
            if self_wait:
                w.append((sem, P.val(sem)))
            return P.op(eng, lambda e, d=dst, s=src: e.dma_start(out=d, in_=s),
                        waits=w, inc=sem, inc_amt=16)

        dma("sync", ident[:], ident_d.ap(), "lds", self_wait=False)
        dma("sync", bu_cols[:], bu_d.ap(), "lds", self_wait=False)
        dma("sync", bbase_col[:], bbase_d.ap(), "lds", self_wait=False)
        dma("sync", gb_cols[:], gb_d.ap(), "lds", self_wait=False)
        dma("sync", rowtmp[:], bv_d.ap(), "lds", self_wait=False)
        dma("sync", rowtmp2[:], bout_d.ap(), "lds", self_wait=False)
        dma("sync", rowtmp3[:], bffn_d.ap(), "lds", self_wait=False)
        dma("sync", s_t[:], sin_d.ap(), "lds", self_wait=False)
        dma("sync", w_t[:], wsc_d.ap(), "lds", self_wait=False)
        ld_small = P.val("lds")

        P.op("vector", lambda e: e.memset(ones_bf[:], 1.0))
        P.op("vector", lambda e: e.memset(ones32[:], 1.0))
        P.op("vector", lambda e: e.tensor_copy(bv_bf[:], rowtmp[:]),
             waits=[("lds", ld_small)])
        P.op("vector", lambda e: e.tensor_scalar_mul(
            gb_cols[:, 0:2], gb_cols[:, 0:2], 1.0 / S))
        P.op("vector", lambda e: e.tensor_tensor(s2_t[:], s_t[:], w_t[:], op=ALU.mult))
        P.op("vector", lambda e: e.tensor_tensor(sc2_t[:], s_t[:], s_t[:], op=ALU.mult))
        P.op("vector", lambda e: e.reciprocal(i2s_t[:], s_t[:]))
        P.op("vector", lambda e: e.drain())
        vv0 = P.op("vector", lambda e: e.tensor_scalar_add(sc2_t[:], sc2_t[:], -0.0225),
                   inc="v")
        sv0 = P.op("scalar", lambda e: e.activation(mask_t[:], sc2_t[:], AF.Sign),
                   waits=[("v", vv0)], inc="s")
        P.op("vector", lambda e: e.tensor_scalar(c_t[:], mask_t[:], 0.25, 0.75,
                                                 op0=ALU.mult, op1=ALU.add),
             waits=[("s", sv0)])
        P.op("vector", lambda e: e.tensor_scalar(rci_t[:], mask_t[:], -0.5, 1.5,
                                                 op0=ALU.mult, op1=ALU.add))
        P.op("vector", lambda e: e.drain())
        P.op("vector", lambda e: e.tensor_tensor(sc2_t[:], s2_t[:], rci_t[:], op=ALU.mult))
        P.op("vector", lambda e: e.tensor_tensor(ci2s_t[:], i2s_t[:], c_t[:], op=ALU.mult))
        P.op("vector", lambda e: e.tensor_scalar_mul(cb_t[:], c_t[:], 448.0))
        P.op("vector", lambda e: e.drain())
        P.op("vector", lambda e: e.reciprocal(isc_t[:], sc2_t[:]))
        P.op("vector", lambda e: e.tensor_scalar_mul(ncb_t[:], cb_t[:], -1.0))
        P.op("vector", lambda e: e.drain())
        P.op("vector", lambda e: e.tensor_scalar_mul(rowtmp3[:], rowtmp3[:], isc_t[:]))
        P.op("vector", lambda e: e.drain())
        v_pre = P.op("vector", lambda e: e.tensor_copy(bffn_bf[:], rowtmp3[:]), inc="v")
        P.op("tensor", lambda e: e.matmul(
            PB[0][:, 0:512], ones32[:], rowtmp2[0:1, 0:512], start=True, stop=True),
            waits=[("v", v_pre), ("lds", ld_small)])
        P.op("tensor", lambda e: e.matmul(
            PB[1][:, 0:512], ones32[:], rowtmp2[0:1, 512:1024], start=True, stop=True))
        P.op("tensor", lambda e: e.matmul(
            pn[0][:, 0:1], ones32[:], sc2_t[:], start=True, stop=True))
        P.op("tensor", lambda e: e.matmul(
            pn[0][:, 1:2], ones32[:], ci2s_t[:], start=True, stop=True))
        P.op("tensor", lambda e: e.matmul(
            pn[0][:, 2:3], ones32[:], cb_t[:], start=True, stop=True))
        pe_bc = P.op("tensor", lambda e: e.matmul(
            pn[0][:, 3:4], ones32[:], ncb_t[:], start=True, stop=True), inc="pe")
        P.op("vector", lambda e: e.tensor_copy(bout_bc[:, 0:512], PB[0][:, 0:512]),
             waits=[("pe", pe_bc)])
        P.op("vector", lambda e: e.tensor_copy(bout_bc[:, 512:1024], PB[1][:, 0:512]))
        P.op("vector", lambda e: e.tensor_copy(scomb_bc[:], pn[0][:, 0:1]))
        P.op("vector", lambda e: e.tensor_copy(inv2s_bc[:], pn[0][:, 1:2]))
        P.op("vector", lambda e: e.tensor_copy(cb_bc[:], pn[0][:, 2:3]))
        v_bc = P.op("vector", lambda e: e.tensor_copy(ncb_bc[:], pn[0][:, 3:4]),
                    inc="v")
        st["pb"][0] = st["pb"][1] = ("v", v_bc)
        st["pb"][4] = ("v", v_bc)

        pre_v = P.val("v")
        scr_ready = {}
        scr_q = [0]

        def scr_store(sc_idx, src_ap, cast_v):
            qi = scr_q[0] % 2
            scr_q[0] += 1
            v = P.op("scalar", lambda e, d=sc_idx, src=src_ap:
                     e.dma_start(out=sc_d.ap()[d][:, 0:src.shape[-1]], in_=src),
                     waits=[("v", cast_v), ("q%d" % qi, P.val("q%d" % qi))],
                     inc="q%d" % qi, inc_amt=16)
            scr_ready[sc_idx] = [("q%d" % qi, v)]

        rA_free = [("v", pre_v), ("pe", pe_bc)]
        rV_free = [("v", pre_v)]
        rU_free = [("v", pre_v)]
        qk_gate = [("v", pre_v)]

        QV, QU, QB, QF = list(range(4)), list(range(4, 8)), 8, [13, 14]

        stg_free = [("v", 0), ("v", 0)]

        def load_panel(s, dst_fn, sc_idx, cols, f32_srcs, wsem, free_w, gate):
            fw = list(free_w) if isinstance(free_w, list) else [free_w]
            if s > 0:
                g = scr_ready[sc_idx] if s == 1 else []
                v = dma("sync", dst_fn(0, cols), sc_d.ap()[sc_idx][:, 0:cols],
                        wsem, waits=fw + g)
                return [(wsem, v)]
            cv = 0
            for i, (src, col, w) in enumerate(f32_srcs):
                sl = i % 2
                ldv = dma("sync", wstage[sl][:, 0:w], src, "w%d" % sl,
                          waits=[stg_free[sl]] + (fw if i == 0 else []))
                cv = P.op("vector", lambda e, sl=sl, col=col, w=w:
                          e.tensor_copy(dst_fn(col, w), wstage[sl][:, 0:w]),
                          waits=[("w%d" % sl, ldv)] + (fw if i == 0 else []),
                          inc="v")
                stg_free[sl] = ("v", cv)
            scr_store(sc_idx, dst_fn(0, cols), cv)
            return [("v", cv)]

        wpre_q = [0]
        wc8_free = [("v", 0)]

        def emit_wout_precast_piece():
            i, k = wpre_q[0] // KC, wpre_q[0] % KC
            wpre_q[0] += 1
            hh, rr = i // 2, i % 2
            row = 9 + i
            sl = wpre_q[0] % 2
            g = "ga%d" % sl
            ldv = P.op("scalar", lambda e, rr=rr, k=k, hh=hh, sl=sl: e.dma_start(
                out=wstage[sl][:],
                in_=wout_d.ap()[(rr * 8 + k) * 128:(rr * 8 + k + 1) * 128,
                                hh * 512:(hh + 1) * 512]),
                waits=[stg_free[sl], (g, P.val(g))], inc=g, inc_amt=16)
            cv = P.op("vector", lambda e, sl=sl: e.tensor_copy(
                wc8s[:], wstage[sl][:]),
                waits=[(g, ldv), wc8_free[0]], inc="v")
            stg_free[sl] = ("v", cv)
            stv = P.op("scalar", lambda e, row=row, k=k, sl=sl: e.dma_start(
                out=sc_d.ap()[row][:, k * 512:(k + 1) * 512],
                in_=wc8s[:]),
                waits=[("v", cv), ("q%d" % sl, P.val("q%d" % sl))],
                inc="q%d" % sl, inc_amt=16)
            wc8_free[0] = ("q%d" % sl, stv)
            if k == KC - 1:
                scr_ready[row] = [("q0", P.val("q0")),
                                  ("q1", P.val("q1"))]
            state["wpre_done"] = [("q0", P.val("q0")), ("q1", P.val("q1"))]

        state = {
            "p3_pe": 0, "p4_pe": 0, "ffn_pe": 0,
            "ln_evac": [("s", 0)] * TC,
            "qk_ready": 0, "v_ready": 0, "u_ready": 0,
            "sq_ready": [0, 0],
        }

        tp3d = p_tp[:].rearrange("p (c t) -> p c t", c=KC)
        tp2_3d = p_tp2[:].rearrange("p (c t) -> p c t", c=KC)

        state["stats1"] = 0

        def emit_P1A_tile(s, t, xgate=()):
            b = t % 2
            ldv = dma("sync", x_t[b][:], x_d.ap()[s, t * 128:(t + 1) * 128, :],
                      "lx%d" % b, waits=[st["xt"][b], ("pe", state["p3_pe"])] +
                      list(xgate))
            P.op("vector", lambda e, t=t, b=b: e.bn_stats(
                stbn[:, t * 12:t * 12 + 6], x_t[b][:, 0:512]),
                waits=[("lx%d" % b, ldv)])
            vv = P.op("vector", lambda e, t=t, b=b: e.bn_stats(
                stbn[:, t * 12 + 6:t * 12 + 12], x_t[b][:, 512:1024]), inc="v")
            st["xt"][b] = ("v", vv)

        pend = {"q_rc": 0, "tail_rc": None}

        def emit_P1A_tail(s, defer=False):
            P.op("vector", lambda e: e.drain())
            for t in range(TC):
                P.op("vector", lambda e, t=t: e.bn_aggr(
                    stmv[:, 2 * t:2 * t + 2], stbn[:, 12 * t:12 * t + 12]))
            P.op("vector", lambda e: e.drain())
            P.op("vector", lambda e: e.tensor_scalar_add(
                ve_all[:], stmv[:].rearrange("p (t c) -> p t c", c=2)[:, :, 1], 1e-5))
            P.op("vector", lambda e: e.drain())
            vv = P.op("vector", lambda e: e.reciprocal(rc_all[:], ve_all[:]), inc="v")
            if defer:
                pend["tail_rc"] = vv
                return
            sv = P.op("scalar", lambda e: e.activation(rs_all[:], rc_all[:], AF.Sqrt),
                      waits=[("v", vv)], inc="s")
            state["stats1"] = P.op("vector", lambda e: e.scalar_tensor_tensor(
                nb_all[:], stmv[:].rearrange("p (t c) -> p t c", c=2)[:, :, 0],
                -1.0, rs_all[:], op0=ALU.mult, op1=ALU.mult),
                waits=[("s", sv)], inc="v")

        ln3d = ln[:].rearrange("p (c t) -> p c t", c=KC)

        p1b_sv = {}

        def emit_P1B_norm(s, t):
            b = t % 2
            hw = [st["lnTh"][0], st["lnTh"][1]] if t % 2 == 0 else []
            if s == 0:
                sv = P.op("scalar", lambda e, t=t: e.activation(
                    lnT_scr[t % 2][:], x1_tile(t), AF.Identity,
                    scale=rs_all[:, t:t + 1], bias=nb_all[:, t:t + 1]),
                    waits=[("v", state["stats1"]), st["lnT"][t % 2]] + hw,
                    inc="s")
                p1b_sv[t] = sv
                return
            ldv = dma("sync", x_t[b][:], x_d.ap()[s, t * 128:(t + 1) * 128, :],
                      "lx%d" % b, waits=[st["xt"][b]])
            sv = P.op("scalar", lambda e, t=t, b=b: e.activation(
                lnT_scr[t % 2][:], x_t[b][:], AF.Identity,
                scale=rs_all[:, t:t + 1], bias=nb_all[:, t:t + 1]),
                waits=[("lx%d" % b, ldv), ("v", state["stats1"]),
                       st["lnT"][t % 2]] + hw,
                inc="s")
            st["xt"][b] = ("s", sv)
            p1b_sv[t] = sv

        def emit_P1B_tp(s, t, rA, pre=False):
            sv = p1b_sv[t]
            odd = pre and t % 2 == 1
            bank, b3d, key = (p_tp, tp3d, "ptp") if odd else \
                             (p_tp2, tp2_3d, "ptp2")
            pv = None
            for hc in range(KC):
                w = [("s", sv), st[key]] if hc == 0 else []
                pv = P.op("tensor", lambda e, hc=hc, t=t, bank=bank: e.transpose(
                    bank[:, hc * 128:(hc + 1) * 128],
                    lnT_scr[t % 2][:, hc * 128:(hc + 1) * 128], ident[:]),
                    waits=w, inc="pe" if hc == KC - 1 else None)
            st["lnT"][t % 2] = ("pe", pv)
            if t % 2 == 0:
                st["lnTh"][0] = st["lnTh"][1] = ("pe", pv)
            if pre:
                ev = P.op("vector", lambda e, t=t, b3d=b3d: e.tensor_copy(
                    ln3d[:, :, t * 128:(t + 1) * 128], b3d),
                    waits=[("pe", pv)] + (rA if t == 0 else []), inc="v")
                st[key] = ("v", ev)
                state["ln_evac"][t] = ("v", ev)
            else:
                ev = P.op("scalar", lambda e, t=t: e.activation(
                    ln3d[:, :, t * 128:(t + 1) * 128], tp2_3d, AF.Copy),
                    waits=[("pe", pv)] + (rA if t == 0 else []), inc="s")
                st[key] = ("s", ev)
                state["ln_evac"][t] = ("s", ev)

        def emit_P1B_tile(s, t, rA, pre=False):
            emit_P1B_norm(s, t)
            emit_P1B_tp(s, t, rA, pre=pre)

        stats2_q = [0] * ((TC + 3) // 4)

        def emit_ln2_quad(q):
            a, bq = 4 * q, min(4 * q + 4, TC)
            P.op("vector", lambda e: e.drain())
            for t in range(a, bq):
                P.op("vector", lambda e, t=t: e.bn_aggr(
                    stmv2[:, 2 * t:2 * t + 2], stbn2[:, 12 * t:12 * t + 12]))
            P.op("vector", lambda e: e.drain())
            P.op("vector", lambda e, a=a, bq=bq: e.tensor_scalar_add(
                ve2_all[:, a:bq],
                stmv2[:, 2 * a:2 * bq].rearrange("p (t c) -> p t c", c=2)[:, :, 1],
                1e-5))
            P.op("vector", lambda e: e.drain())
            pend["q_rc"] = P.op("vector", lambda e, a=a, bq=bq: e.reciprocal(
                rc2_all[:, a:bq], ve2_all[:, a:bq]), inc="v")

        def emit_stats_finish(s):
            sv = P.op("scalar", lambda e: e.activation(
                rs2_all[:, 0:TC], rc2_all[:, 0:TC], AF.Sqrt),
                waits=[("v", pend["q_rc"])], inc="s")
            sv2 = None
            if pend["tail_rc"] is not None:
                sv2 = P.op("scalar", lambda e: e.activation(
                    rs_all[:], rc_all[:], AF.Sqrt),
                    waits=[("v", pend["tail_rc"])], inc="s")
            P.op("vector", lambda e: e.tensor_scalar(
                rs2s_all[:, 0:TC], rs2_all[:, 0:TC], inv2s_bc[:], 0.0,
                op0=ALU.mult, op1=ALU.add), waits=[("s", sv)])
            P.op("vector", lambda e: e.drain())
            qv = P.op("vector", lambda e: e.scalar_tensor_tensor(
                nb2_all[:, 0:TC],
                stmv2[:, 0:2 * TC].rearrange("p (t c) -> p t c", c=2)[:, :, 0],
                -1.0, rs2s_all[:, 0:TC], op0=ALU.mult, op1=ALU.mult), inc="v")
            for q in range(len(stats2_q)):
                stats2_q[q] = qv
            if sv2 is not None:
                state["stats1"] = P.op("vector", lambda e: e.scalar_tensor_tensor(
                    nb_all[:], stmv[:].rearrange("p (t c) -> p t c", c=2)[:, :, 0],
                    -1.0, rs_all[:], op0=ALU.mult, op1=ALU.mult),
                    waits=[("s", sv2)], inc="v")
                pend["tail_rc"] = None

        def emit_P1(s, rA, xgate=()):
            if s == 0:
                for t in range(TC):
                    dma("sync", x1_tile(t),
                        x_d.ap()[0, t * 128:(t + 1) * 128, :],
                        "lx0", self_wait=False)
                allx = P.val("lx0")
                for t in range(TC):
                    P.op("vector", lambda e, t=t: e.bn_stats(
                        stbn[:, t * 12:t * 12 + 6], x1_tile(t)[:, 0:512]),
                        waits=[("lx0", allx)])
                    P.op("vector", lambda e, t=t: e.bn_stats(
                        stbn[:, t * 12 + 6:t * 12 + 12],
                        x1_tile(t)[:, 512:1024]))
            else:
                for t in range(TC):
                    emit_P1A_tile(s, t, xgate)
            emit_P1A_tail(s)
            for t in range(TC):
                emit_P1B_tile(s, t, rA, pre=True)

        def make_scores(j, sqi):
            ops = []
            for mc in range(TC):
                def emit(mc=mc, j=j, sqi=sqi):
                    b = mc % 2
                    pv = P.op("tensor", lambda e, mc=mc, j=j, b=b: e.matmul(
                        p_mm[b][:, 0:NT],
                        qk[:, S + mc * 128:S + (mc + 1) * 128],
                        qk[:, j * NT:(j + 1) * NT],
                        start=True, stop=True),
                        waits=[("s", state["qk_ready"]), st["pb"][b]],
                        inc="pe")
                    sv = P.op("scalar", lambda e, mc=mc, b=b: e.activation(
                        t_relu[b], p_mm[b][:, 0:NT], AF.Relu),
                        waits=[("pe", pv), st["trelu"][b], st["lnTh"][b]],
                        inc="s")
                    st["pb"][b] = ("s", sv)
                    vv = P.op("vector", lambda e, mc=mc, sqi=sqi, b=b: e.tensor_tensor(
                        sq_b[sqi][:, mc * NT:(mc + 1) * NT], t_relu[b],
                        t_relu[b], op=ALU.mult),
                        waits=[("s", sv)], inc="v")
                    st["trelu"][b] = ("v", vv)
                    st["lnTh"][b] = ("v", vv)
                    state["sq_ready"][sqi] = vv
                ops.append(emit)
            return ops

        pending_scores = []

        def drip_scores(n=1):
            for _ in range(n):
                if pending_scores:
                    pending_scores.pop(0)()

        emit_P1(0, rA_free, xgate=[("v", pre_v)])
        for s in range(seqs):
            ln_evac = list(state["ln_evac"])
            sq0_dead_pe = 0

            wvbw = state.pop("wvbw_next", None)
            if wvbw is None:
                wvbw = load_panel(
                    s, lambda col, w: wbase[:, col:col + w], QB, KC * 128,
                    [(wuv_d.ap()[k * 128:(k + 1) * 128, 2 * E:2 * E + SD],
                      k * 128, SD) for k in range(KC)],
                    "w0", st["wbase"], [])
            for bt in range(NBT):
                pi = rot[0] % 4
                rot[0] += 1
                tb0 = (bt * BT) // 128
                for k in range(KC):
                    w = wvbw + list(ln_evac[tb0:tb0 + BT // 128]) + \
                        [st["pb"][pi]] if k == 0 else []
                    P.op("tensor", lambda e, k=k, bt=bt, pi=pi: e.matmul(
                        PB[pi][:, 0:BT],
                        wbase[:, k * 128:(k + 1) * 128],
                        ln[:, k * S + bt * BT:k * S + (bt + 1) * BT],
                        start=(k == 0), stop=(k == KC - 1)),
                        waits=w, inc="pe" if k == KC - 1 else None)
                pv = P.val("pe")
                sv = P.op("scalar", lambda e, bt=bt, pi=pi: e.activation(
                    base_t[:, bt * BT:(bt + 1) * BT], PB[pi][:, 0:BT], AF.Silu,
                    bias=bbase_col[:]),
                    waits=[("pe", pv)] + (rU_free if bt == 0 else []), inc="s")
                st["pb"][pi] = ("s", sv)
            base_pe = P.val("pe")
            st["wbase"] = [("pe", base_pe)] + (scr_ready[QB] if s == 0 else [])
            P.op("scalar", lambda e: e.drain())
            P.op("scalar", lambda e: e.activation(
                qk[:, 0:S], base_t[:], AF.Identity,
                scale=gb_cols[:, 0:1], bias=gb_cols[:, 1:2]),
                waits=qk_gate)
            state["qk_ready"] = P.op("scalar", lambda e: e.activation(
                qk[:, S:2 * S], base_t[:], AF.Identity,
                scale=gb_cols[:, 2:3], bias=gb_cols[:, 3:4]), inc="s")
            P.op("scalar", lambda e: e.drain())

            for qi in range(4):
                wsl = qi % 2
                wvw = load_panel(
                    s, lambda col, w, wsl=wsl: wbuf[wsl][:, col:col + w],
                    QV[qi], 4096,
                    [(wuv_d.ap()[k * 128:(k + 1) * 128,
                                 E + qi * 512:E + (qi + 1) * 512], k * 512, 512)
                     for k in range(KC)],
                    "w%d" % wsl, st["wbuf"][wsl], [])
                last_pe = None
                for t in range(TC):
                    pi = rot[0] % 4
                    rot[0] += 1
                    for k in range(KC):
                        w = wvw + [ln_evac[t], st["pb"][pi]] \
                            if k == 0 else []
                        P.op("tensor", lambda e, k=k, t=t, pi=pi, wsl=wsl: e.matmul(
                            PB[pi][:, 0:512],
                            ln[:, k * S + t * 128:k * S + (t + 1) * 128],
                            wbuf[wsl][:, k * 512:(k + 1) * 512],
                            start=(k == 0), stop=False), waits=w)
                    pv = P.op("tensor", lambda e, q=qi, pi=pi: e.matmul(
                        PB[pi][:, 0:512], ones_bf[:],
                        bv_bf[0:1, q * 512:(q + 1) * 512],
                        start=False, stop=True), inc="pe")
                    last_pe = pv
                    sv = P.op("scalar", lambda e, t=t, q=qi, pi=pi: e.activation(
                        v_t[:, t * E + q * 512:t * E + (q + 1) * 512],
                        PB[pi][:, 0:512], AF.Silu),
                        waits=[("pe", pv)] + (rV_free if (t == 0 and qi == 0) else []),
                        inc="s")
                    st["pb"][pi] = ("s", sv)
                st["wbuf"][wsl] = [("pe", last_pe)] + \
                    (scr_ready[QV[qi]] if s == 0 else [])
            state["v_ready"] = P.val("s")

            mm_count = 0
            for qi in range(4):
                wsl = qi % 2
                wvw = load_panel(
                    s, lambda col, w, wsl=wsl: wbuf[wsl][:, col:col + w],
                    QU[qi], 4096,
                    [(wuv_d.ap()[k * 128:(k + 1) * 128,
                                 qi * 512:(qi + 1) * 512], k * 512, 512)
                     for k in range(KC)],
                    "w%d" % wsl, st["wbuf"][wsl], [])
                last_pe = None
                for es in range(4):
                    ec = qi * 4 + es
                    for ntj in range(NNT):
                        pi = rot[0] % 4
                        rot[0] += 1
                        t_lo, t_hi = (ntj * NT) // 128, ((ntj + 1) * NT - 1) // 128
                        for k in range(KC):
                            w = wvw + list(ln_evac[t_lo:t_hi + 1]) + \
                                [st["pb"][pi]] if k == 0 else []
                            P.op("tensor", lambda e, k=k, es=es, ntj=ntj, pi=pi, wsl=wsl: e.matmul(
                                PB[pi][:, 0:NT],
                                wbuf[wsl][:, k * 512 + es * 128:k * 512 + (es + 1) * 128],
                                ln[:, k * S + ntj * NT:k * S + (ntj + 1) * NT],
                                start=(k == 0), stop=(k == KC - 1)),
                                waits=w, inc="pe" if k == KC - 1 else None)
                        pv = P.val("pe")
                        last_pe = pv
                        sv = P.op("scalar", lambda e, ec=ec, ntj=ntj, pi=pi: e.activation(
                            u_t[:, ec * S + ntj * NT:ec * S + (ntj + 1) * NT],
                            PB[pi][:, 0:NT], AF.Silu, bias=bu_cols[:, ec:ec + 1]),
                            waits=[("pe", pv)], inc="s")
                        st["pb"][pi] = ("s", sv)
                        mm_count += 1
                st["wbuf"][wsl] = [("pe", last_pe)] + \
                    (scr_ready[QU[qi]] if s == 0 else [])
            state["u_ready"] = P.val("s")

            wf0 = load_panel(
                s, lambda col, w: wbuf[0][:, col:col + w], QF[0], 4096,
                [(wffnT_d.ap()[k * 128:(k + 1) * 128, h * 512:(h + 1) * 512],
                  (k % 4) * 1024 + h * 512, 512)
                 for k in range(4) for h in range(2)],
                "w0", st["wbuf"][0], [])
            wf1 = load_panel(
                s, lambda col, w: wbuf[1][:, col:col + w], QF[1], 4096,
                [(wffnT_d.ap()[k * 128:(k + 1) * 128, h * 512:(h + 1) * 512],
                  ((k - 4) % 4) * 1024 + h * 512, 512)
                 for k in range(4, 8) for h in range(2)],
                "w1", st["wbuf"][1], [])

            p1a_q = list(range(TC)) if s + 1 < seqs else []
            p1a_tail_due = s + 1 < seqs

            def drip_p1a(gate=()):
                if p1a_q:
                    emit_P1A_tile(s + 1, p1a_q.pop(0), xgate=gate)

            h_done_v = 0
            for j in range(NNT):
                sqi = j % 2
                own_scores = make_scores(j, sqi) if j == 0 else []
                sq_mc_v = [0] * TC
                if j == 0:
                    for k in range(min(3, TC)):
                        own_scores[k]()
                        sq_mc_v[k] = state["sq_ready"][sqi]
                if j + 1 < NNT:
                    pending_scores = make_scores(j + 1, (j + 1) % 2)
                else:
                    pending_scores = []
                for eq in range(4):
                    es_pe = [0] * 4
                    if j == 0 and eq == 0:
                        for mc in range(TC):
                            for idx in (2 * mc + 3, 2 * mc + 4):
                                if idx < TC:
                                    own_scores[idx]()
                                    sq_mc_v[idx] = state["sq_ready"][sqi]
                            for es in range(4):
                                w = [("v", sq_mc_v[mc])]
                                if mc == 0:
                                    w += [("s", state["v_ready"]), st["pb"][2 + es]]
                                P.op("tensor", lambda e, eq=eq, mc=mc, es=es, sqi=sqi: e.matmul(
                                    PB[2 + es][:, 0:NT],
                                    v_t[:, mc * E + eq * 512 + es * 128:
                                        mc * E + eq * 512 + (es + 1) * 128],
                                    sq_b[sqi][:, mc * NT:(mc + 1) * NT],
                                    start=(mc == 0), stop=(mc == TC - 1)),
                                    waits=w,
                                    inc="pe" if mc == TC - 1 else None)
                                if mc == TC - 1:
                                    es_pe[es] = P.val("pe")
                    else:
                        for es in range(4):
                            for mc in range(TC):
                                w = []
                                if mc == 0:
                                    w = [("v", state["sq_ready"][sqi]),
                                         ("s", state["v_ready"]), st["pb"][2 + es]]
                                P.op("tensor", lambda e, eq=eq, mc=mc, es=es, sqi=sqi: e.matmul(
                                    PB[2 + es][:, 0:NT],
                                    v_t[:, mc * E + eq * 512 + es * 128:
                                        mc * E + eq * 512 + (es + 1) * 128],
                                    sq_b[sqi][:, mc * NT:(mc + 1) * NT],
                                    start=(mc == 0), stop=(mc == TC - 1)),
                                    waits=w,
                                    inc="pe" if mc == TC - 1 else None)
                            es_pe[es] = P.val("pe")
                            drip_scores(1)
                            ec = eq * 4 + es
                            vv = P.op("vector", lambda e, ec=ec, es=es, j=j: e.tensor_tensor(
                                u_t[:, ec * S + j * NT:ec * S + (j + 1) * NT],
                                PB[2 + es][:, 0:NT],
                                u_t[:, ec * S + j * NT:ec * S + (j + 1) * NT],
                                op=ALU.mult),
                                waits=[("pe", es_pe[es]), ("s", state["u_ready"])],
                                inc="v")
                            st["pb"][2 + es] = ("v", vv)
                            h_done_v = vv
                            drip_scores(1)
                            if s == 0 and wpre_q[0] < 4 * KC:
                                emit_wout_precast_piece()
                    if j == 0 and eq == 0:
                        for es in range(4):
                            ec = eq * 4 + es
                            vv = P.op("vector", lambda e, ec=ec, es=es, j=j: e.tensor_tensor(
                                u_t[:, ec * S + j * NT:ec * S + (j + 1) * NT],
                                PB[2 + es][:, 0:NT],
                                u_t[:, ec * S + j * NT:ec * S + (j + 1) * NT],
                                op=ALU.mult),
                                waits=[("pe", es_pe[es]), ("s", state["u_ready"])], inc="v")
                            st["pb"][2 + es] = ("v", vv)
                            h_done_v = vv
                    if j == NNT - 1 and j > 0:
                        drip_p1a(gate=[("pe", P.val("pe"))])
                drip_scores(TC)
                if sqi == 0:
                    sq0_dead_pe = P.val("pe")
            h_ready = h_done_v
            if s == 0:
                while wpre_q[0] < 4 * KC:
                    emit_wout_precast_piece()
            p3_pe = P.val("pe")
            state["p3_pe"] = p3_pe

            wo_w = [None, None]
            for i in range(4):
                gate = sq0_dead_pe if i < 2 and NNT >= 2 else p3_pe
                hh, rr = i // 2, i % 2
                load_panel(
                    max(s, 1), lambda col, w, i=i: wout_big[:, i * 4096 + col:
                                                            i * 4096 + col + w],
                    9 + i, 4096, [],
                    "w%d" % (i % 2), ("pe", gate), [])
                if i == 1:
                    wo_w[0] = [("w0", P.val("w0")), ("w1", P.val("w1")),
                               ("v", P.val("v"))]
            wo_w[1] = [("w0", P.val("w0")), ("w1", P.val("w1")),
                       ("v", P.val("v"))]
            x1_pre = {}

            def emit_x1pre(t):
                b = t % 2
                ldv = dma("sync", x_t[b][:], x_d.ap()[s, t * 128:(t + 1) * 128, :],
                          "lx%d" % b, waits=[st["xt"][b], ("pe", p3_pe)])
                vv = P.op("vector", lambda e, t=t, b=b: e.scalar_tensor_tensor(
                    x1_tile(t), x_t[b][:], 1.0, bout_bc[:],
                    op0=ALU.mult, op1=ALU.add),
                    waits=[("lx%d" % b, ldv), ("pe", p3_pe)], inc="v")
                st["xt"][b] = ("v", vv)
                x1_pre[t] = ("v", vv)

            for t in range(min(2, TC)):
                emit_x1pre(t)
            for t in range(TC):
                wi = t % 2
                x1ld = x1_pre[t]
                for hh in range(2):
                    for ec in range(EC):
                        w = []
                        if ec == 0:
                            w = wo_w[hh] + [("v", h_ready),
                                            st["pb"][2 * wi], st["pb"][2 * wi + 1]]
                        P.op("tensor", lambda e, ec=ec, t=t, hh=hh, wi=wi: e.matmul(
                            pw[wi][:, hh * 512:(hh + 1) * 512],
                            u_t[:, ec * S + t * 128:ec * S + (t + 1) * 128],
                            wout_big[:, hh * 8192 + (ec // 8) * 4096 +
                                     (ec % 8) * 512:
                                     hh * 8192 + (ec // 8) * 4096 + (ec % 8) * 512 + 512],
                            start=(ec == 0), stop=(ec == EC - 1)),
                            waits=w, inc="pe" if (ec == EC - 1 and hh == 1) else None)
                pv = P.val("pe")
                vv = P.op("vector", lambda e, t=t, wi=wi: e.tensor_tensor(
                    x1_tile(t), pw[wi][:, 0:1024], x1_tile(t), op=ALU.add),
                    waits=[("pe", pv), x1ld], inc="v")
                st["pb"][2 * wi] = st["pb"][2 * wi + 1] = ("v", vv)
                x1_done = P.val("v")
                P.op("vector", lambda e, t=t: e.bn_stats(
                    stbn2[:, t * 12:t * 12 + 6], x1_tile(t)[:, 0:512]),
                    waits=[("v", x1_done)])
                P.op("vector", lambda e, t=t: e.bn_stats(
                    stbn2[:, t * 12 + 6:t * 12 + 12], x1_tile(t)[:, 512:1024]))
                if t + 2 < TC:
                    emit_x1pre(t + 2)
                drip_p1a()
                if p1a_tail_due and not p1a_q:
                    emit_P1A_tail(s + 1, defer=True)
                    p1a_tail_due = False
                if t % 4 == 0 and t > 0:
                    emit_ln2_quad(t // 4 - 1)
                if t == TC - 1:
                    emit_ln2_quad(t // 4)
            rA_next = [("pe", P.val("pe"))]
            p4_pe = P.val("pe")
            state["p4_pe"] = p4_pe
            emit_stats_finish(s)

            yf3d = y_f[:].rearrange("p (c t) -> p c t", c=KC)
            y_evac = [0] * TC
            y_evac2 = [0] * TC
            th_v = [0] * TC
            q_done = [0] * TC

            def emit_th(t):
                b = t % 2
                th_v[t] = P.op("scalar", lambda e, t=t, b=b: e.activation(
                    th_s[b][:], x1_tile(t), AF.Identity,
                    scale=rs2s_all[:, t:t + 1], bias=nb2_all[:, t:t + 1]),
                    waits=[("v", stats2_q[t // 4]), st["th"][b], ("pe", p4_pe)],
                    inc="s")

            def emit_quant(t):
                b = t % 2
                vv = P.op("vector", lambda e, b=b: e.tensor_scalar(
                    tc_s[b][:], th_s[b][:], cb_bc[:], ncb_bc[:],
                    op0=ALU.min, op1=ALU.max),
                    waits=[("s", th_v[t]), st["tcs"][b]], inc="v")
                st["th"][b] = ("v", vv)
                sv = P.op("scalar", lambda e, b=b: e.activation(
                    y8_s[b][:], tc_s[b][:], AF.Copy),
                    waits=[("v", vv), st["y8"][b]], inc="s")
                st["tcs"][b] = ("s", sv)
                vv = P.op("vector", lambda e, b=b: e.tensor_copy(
                    y16_s[b][:], y8_s[b][:]),
                    waits=[("s", sv), st["y16"][b]], inc="v")
                st["y8"][b] = ("v", vv)
                q_done[t] = vv

            pending_out = None

            def emit_out_stt(t, wi, ob, pv):
                vv = P.op("vector", lambda e, t=t, wi=wi, ob=ob: e.scalar_tensor_tensor(
                    out_t[ob][:], pw[wi][:, 0:1024], scomb_bc[:],
                    x1_tile(t), op0=ALU.mult, op1=ALU.add),
                    waits=[("pe", pv), st["outt"][ob]], inc="v")
                st["pb"][2 * wi] = st["pb"][2 * wi + 1] = ("v", vv)
                stv = dma("sync", out_d.ap()[s, t * 128:(t + 1) * 128, :], out_t[ob][:],
                          "o%d" % ob, waits=[("v", P.val("v"))])
                st["outt"][ob] = ("o%d" % ob, stv)

            emit_th(0)
            if TC > 1:
                emit_th(1)
            emit_quant(0)
            if s + 1 < seqs:
                emit_P1B_norm(s + 1, 0)
            for t in range(TC):
                b = t % 2
                pv = None
                for hc in range(KC):
                    w = [("v", q_done[t]), st["ptp"]] if hc == 0 else []
                    pv = P.op("tensor", lambda e, hc=hc, b=b: e.transpose(
                        p_tp[:, hc * 128:(hc + 1) * 128],
                        y16_s[b][:, hc * 128:(hc + 1) * 128], ident[:]),
                        waits=w, inc="pe" if hc == KC - 1 else None)
                st["y16"][b] = ("pe", pv)
                vv = P.op("vector", lambda e, t=t: e.tensor_copy(
                    yf3d[:, 0:KC // 2, t * 128:(t + 1) * 128],
                    tp3d[:, 0:KC // 2, :]),
                    waits=[("pe", pv)] +
                          (state.get("wpre_done", [])
                           if s == 0 and t == 0 else []), inc="v")
                y_evac[t] = vv
                vv = P.op("vector", lambda e, t=t: e.tensor_copy(
                    yf3d[:, KC // 2:KC, t * 128:(t + 1) * 128],
                    tp3d[:, KC // 2:KC, :]), inc="v")
                st["ptp"] = ("v", vv)
                y_evac2[t] = vv
                if t + 1 < TC:
                    emit_quant(t + 1)
                if t == min(8, TC - 1) and s + 1 < seqs:
                    state["wvbw_next"] = load_panel(
                        s + 1, lambda col, w: wbase[:, col:col + w],
                        QB, KC * 128, [], "w0", st["wbase"], [])
                if s + 1 < seqs:
                    emit_P1B_tp(s + 1, t, rA_next)
                    if t + 1 < TC:
                        emit_P1B_norm(s + 1, t + 1)
                if t + 2 < TC:
                    emit_th(t + 2)
                if pending_out is not None:
                    emit_out_stt(*pending_out)
                ob = t % 2
                wi = t % 2
                for ot in range(2):
                    for k in range(KC):
                        w = []
                        if k == 0:
                            w = wf0 + wf1 + [("v", y_evac[t]),
                                 st["pb"][2 * wi], st["pb"][2 * wi + 1]]
                        elif k == KC // 2:
                            w = [("v", y_evac2[t])]
                        P.op("tensor", lambda e, k=k, t=t, ot=ot, wi=wi: e.matmul(
                            pw[wi][:, ot * 512:(ot + 1) * 512],
                            y_f[:, k * S + t * 128:k * S + (t + 1) * 128],
                            wbuf[k // 4][:, (k % 4) * 1024 + ot * 512:
                                         (k % 4) * 1024 + (ot + 1) * 512],
                            start=(k == 0), stop=False), waits=w)
                    P.op("tensor", lambda e, ot=ot, wi=wi: e.matmul(
                        pw[wi][:, ot * 512:(ot + 1) * 512], ones_bf[:],
                        bffn_bf[0:1, ot * 512:(ot + 1) * 512],
                        start=False, stop=True),
                        inc="pe" if ot == 1 else None)
                pending_out = (t, wi, ob, P.val("pe"))
            emit_out_stt(*pending_out)
            pending_out = None
            ffn_pe = P.val("pe")
            st["wbuf"][0] = [("pe", ffn_pe)] + (scr_ready[QF[0]] if s == 0 else [])
            st["wbuf"][1] = [("pe", ffn_pe)] + (scr_ready[QF[1]] if s == 0 else [])

            rV_free = [("v", P.val("v")), ("s", P.val("s"))]
            rU_free = [("o0", P.val("o0")), ("o1", P.val("o1")), ("v", P.val("v")),
                       ("pe", ffn_pe), ("s", P.val("s"))]
            qk_gate = []
            P.op("vector", lambda e: e.drain())
            P.op("scalar", lambda e: e.drain())

        P.op("sync", lambda e: e.nop(),
             waits=[("o0", P.val("o0")), ("o1", P.val("o1"))])

        with nc.Block() as block:
            def body(eng_name):
                def f(eng):
                    for waits, fn, inc, inc_amt in P.progs[eng_name]:
                        for sn, v in waits:
                            eng.wait_ge(sems[sn], v)
                        r = fn(eng)
                        if inc is not None:
                            r.then_inc(sems[inc], inc_amt)
                return f
            block.sync(body("sync"))
            block.scalar(body("scalar"))
            block.vector(body("vector"))
            block.tensor(body("tensor"))
            block.gpsimd(body("gpsimd"))

    return nc


def prep_aux(L0_attn_W_uv, L0_attn_b_uv, L0_attn_gamma, L0_attn_beta,
             L0_attn_W_out, L0_attn_b_out, L0_ffn_weight_fp8, L0_ffn_bias,
             L0_ffn_input_scale, L0_ffn_weight_scale):
    gb = np.stack([
        np.asarray(L0_attn_gamma)[0], np.asarray(L0_attn_beta)[0],
        np.asarray(L0_attn_gamma)[1], np.asarray(L0_attn_beta)[1],
    ], axis=1).astype(np.float32)
    b_uv = np.asarray(L0_attn_b_uv, dtype=np.float32)
    return {
        "wuv": np.ascontiguousarray(np.asarray(L0_attn_W_uv, dtype=np.float32)),
        "wout": np.ascontiguousarray(np.asarray(L0_attn_W_out, dtype=np.float32)),
        "wffnT": np.ascontiguousarray(np.asarray(L0_ffn_weight_fp8, dtype=np.float32).T),
        "ident": np.eye(128, dtype=ml_dtypes.bfloat16),
        "bu_cols": np.ascontiguousarray(b_uv[0:E].reshape(EC, 128).T),
        "bbase_col": np.ascontiguousarray(b_uv[2 * E:2 * E + SD].reshape(128, 1)),
        "gb_cols": np.ascontiguousarray(gb),
        "bv_row": np.ascontiguousarray(b_uv[E:2 * E].reshape(1, E)),
        "bout_row": np.ascontiguousarray(
            np.asarray(L0_attn_b_out, dtype=np.float32).reshape(1, H)),
        "bffn_row": np.ascontiguousarray(
            np.asarray(L0_ffn_bias, dtype=np.float32).reshape(1, H)),
        "s_in": np.asarray(L0_ffn_input_scale, dtype=np.float32).reshape(1, 1),
        "w_scale": np.asarray(L0_ffn_weight_scale, dtype=np.float32).reshape(1, 1),
    }


def kernel(L0_attn_inputs, L0_attn_W_uv, L0_attn_b_uv, L0_attn_gamma,
           L0_attn_beta, L0_attn_W_out, L0_attn_b_out, L0_ffn_weight_fp8,
           L0_ffn_bias, L0_ffn_input_scale, L0_ffn_weight_scale):
    n_cores = 8
    x = np.ascontiguousarray(np.asarray(L0_attn_inputs, dtype=np.float32))
    B, S, _ = x.shape
    seqs = B // n_cores
    aux = prep_aux(L0_attn_W_uv, L0_attn_b_uv, L0_attn_gamma, L0_attn_beta,
                   L0_attn_W_out, L0_attn_b_out, L0_ffn_weight_fp8,
                   L0_ffn_bias, L0_ffn_input_scale, L0_ffn_weight_scale)
    nc = build_program(seqs, S)
    in_maps = []
    for c in range(n_cores):
        m = {"x": x[c * seqs:(c + 1) * seqs]}
        m.update(aux)
        in_maps.append(m)
    res = run_bass_kernel_spmd(nc, in_maps, core_ids=list(range(n_cores)))
    return np.concatenate([r["out"] for r in res.results], axis=0)


# version 40 (final)
# speedup vs baseline: 1.0244x; 1.0222x over previous
import numpy as np
import ml_dtypes
import contextlib
import concourse.bass as bass
import concourse.mybir as mybir
from concourse.bass_utils import run_bass_kernel_spmd

F32 = mybir.dt.float32
BF16 = mybir.dt.bfloat16
FP8 = mybir.dt.float8e4
AF = mybir.ActivationFunctionType
ALU = mybir.AluOpType

H = 1024
E = 2048
SD = 128
KC = H // 128
EC = E // 128

ENGS = ("sync", "scalar", "vector", "tensor", "gpsimd")


class Plan:

    def __init__(self):
        self.progs = {e: [] for e in ENGS}
        self.sem_cnt = {n: 0 for n in (
            "lds", "lx0", "lx1", "w0", "w1", "o0", "o1",
            "p0", "p1", "q0", "q1", "ga0", "ga1", "pe", "v", "s", "g")}
        self.waited = {e: {} for e in ENGS}

    def val(self, name):
        return self.sem_cnt[name]

    def op(self, eng, fn, waits=(), inc=None, inc_amt=1):
        real = []
        for sn, v in waits:
            if v <= 0 or self.waited[eng].get(sn, 0) >= v:
                continue
            self.waited[eng][sn] = v
            real.append((sn, v))
        post = None
        if inc is not None:
            self.sem_cnt[inc] += inc_amt
            post = self.sem_cnt[inc]
        self.progs[eng].append((real, fn, inc, inc_amt))
        return post


def build_program(seqs, S):
    TC = S // 128
    NT = min(512, S)
    NNT = S // NT
    BT = min(512, S)
    NBT = S // BT

    nc = bass.Bass("TRN2", target_bir_lowering=False)

    x_d = nc.dram_tensor("x", [seqs, S, H], F32, kind="ExternalInput")
    wuv_d = nc.dram_tensor("wuv", [H, 2 * E + SD], F32, kind="ExternalInput")
    wout_d = nc.dram_tensor("wout", [E, H], F32, kind="ExternalInput")
    wffnT_d = nc.dram_tensor("wffnT", [H, H], F32, kind="ExternalInput")
    ident_d = nc.dram_tensor("ident", [128, 128], BF16, kind="ExternalInput")
    bu_d = nc.dram_tensor("bu_cols", [128, EC], F32, kind="ExternalInput")
    bbase_d = nc.dram_tensor("bbase_col", [128, 1], F32, kind="ExternalInput")
    gb_d = nc.dram_tensor("gb_cols", [128, 4], F32, kind="ExternalInput")
    bv_d = nc.dram_tensor("bv_row", [1, E], F32, kind="ExternalInput")
    bout_d = nc.dram_tensor("bout_row", [1, H], F32, kind="ExternalInput")
    bffn_d = nc.dram_tensor("bffn_row", [1, H], F32, kind="ExternalInput")
    sin_d = nc.dram_tensor("s_in", [1, 1], F32, kind="ExternalInput")
    wsc_d = nc.dram_tensor("w_scale", [1, 1], F32, kind="ExternalInput")
    out_d = nc.dram_tensor("out", [seqs, S, H], F32, kind="ExternalOutput")

    NSC = 15
    sc_d = nc.dram_tensor("wscratch", [NSC, 128, 4096], BF16)

    P = Plan()

    with contextlib.ExitStack() as stack:
        En = stack.enter_context

        wbuf = [En(nc.sbuf_tensor(f"wbuf{i}", [128, 4096], BF16)) for i in range(2)]
        wbase = En(nc.sbuf_tensor("wbase", [128, KC * 128], BF16))
        ones_bf = En(nc.sbuf_tensor("ones_bf", [1, 128], BF16))
        bv_bf = En(nc.sbuf_tensor("bv_bf", [1, E], BF16))
        bffn_bf = En(nc.sbuf_tensor("bffn_bf", [1, H], BF16))
        ones32 = En(nc.sbuf_tensor("ones32", [1, 128], F32))
        bout_bc = En(nc.sbuf_tensor("bout_bc", [128, H], F32))
        ident = En(nc.sbuf_tensor("identt", [128, 128], BF16))
        bu_cols = En(nc.sbuf_tensor("bu_cols_t", [128, EC], F32))
        bbase_col = En(nc.sbuf_tensor("bbase_col_t", [128, 1], F32))
        gb_cols = En(nc.sbuf_tensor("gb_cols_t", [128, 4], F32))
        scomb_bc = En(nc.sbuf_tensor("scomb_bc", [128, 1], F32))
        inv2s_bc = En(nc.sbuf_tensor("inv2s_bc", [128, 1], F32))
        s_t = En(nc.sbuf_tensor("s_t", [1, 1], F32))
        w_t = En(nc.sbuf_tensor("w_t", [1, 1], F32))
        s2_t = En(nc.sbuf_tensor("s2_t", [1, 1], F32))
        sc2_t = En(nc.sbuf_tensor("sc2_t", [1, 1], F32))
        isc_t = En(nc.sbuf_tensor("isc_t", [1, 1], F32))
        i2s_t = En(nc.sbuf_tensor("i2s_t", [1, 1], F32))
        mask_t = En(nc.sbuf_tensor("mask_t", [1, 1], F32))
        c_t = En(nc.sbuf_tensor("c_t", [1, 1], F32))
        rci_t = En(nc.sbuf_tensor("rci_t", [1, 1], F32))
        cb_t = En(nc.sbuf_tensor("cb_t", [1, 1], F32))
        ncb_t = En(nc.sbuf_tensor("ncb_t", [1, 1], F32))
        ci2s_t = En(nc.sbuf_tensor("ci2s_t", [1, 1], F32))
        cb_bc = En(nc.sbuf_tensor("cb_bc", [128, 1], F32))
        ncb_bc = En(nc.sbuf_tensor("ncb_bc", [128, 1], F32))
        stbn = En(nc.sbuf_tensor("stbn", [128, TC * 12], F32))
        stmv = En(nc.sbuf_tensor("stmv", [128, TC * 2], F32))
        ve_all = En(nc.sbuf_tensor("ve_all", [128, TC], F32))
        rc_all = En(nc.sbuf_tensor("rc_all", [128, TC], F32))
        rs_all = En(nc.sbuf_tensor("rs_all", [128, TC], F32))
        nb_all = En(nc.sbuf_tensor("nb_all", [128, TC], F32))
        rs2s_all = En(nc.sbuf_tensor("rs2s_all", [128, TC], F32))
        nb2_all = En(nc.sbuf_tensor("nb2_all", [128, TC], F32))
        ssum = En(nc.sbuf_tensor("ssum", [128, TC], F32))
        ssq = En(nc.sbuf_tensor("ssq", [128, TC], F32))
        mean_all = En(nc.sbuf_tensor("mean_all", [128, TC], F32))
        rs2_all = En(nc.sbuf_tensor("rs2_all", [128, TC], F32))
        ve2_all = En(nc.sbuf_tensor("ve2_all", [128, TC], F32))
        rc2_all = En(nc.sbuf_tensor("rc2_all", [128, TC], F32))
        lnT0 = En(nc.sbuf_tensor("lnT0", [128, H], BF16))
        wstage = [En(nc.sbuf_tensor(f"wstage{i}", [128, 512], F32)) for i in range(2)]
        wc8s = En(nc.sbuf_tensor("wc8s", [128, 512], BF16))

        A0 = (nc.sbuf_base + 127) & ~127
        sz_sq = TC * NT * 2
        sz_A = max(KC * S * 2, 2 * sz_sq, 32768)
        sz_QK = max(2 * S * 2, 2 * H * 4)
        sz_V = max(EC * S * 2, TC * H * 4)
        sz_U = max(EC * S * 2, 32768 + max(KC * S * 2, 24576))
        oA, oQK = A0, A0 + sz_A
        oV, oU = A0 + sz_A + sz_QK, A0 + sz_A + sz_QK + sz_V
        oYF = oU + 32768
        assert oU + sz_U <= nc.sbuf_top, (oU + sz_U, nc.sbuf_top)

        AA = nc.alloc_sbuf_tensor_at
        ln = AA("ln", [128, KC * S], BF16, offset=oA)
        sq_b = [AA(f"sq{i}", [128, TC * NT], BF16, offset=oA + i * sz_sq)
                for i in range(2)]
        wout_big = AA("woutb", [128, 16384], BF16, offset=oA)
        qk = AA("qk", [128, 2 * S], BF16, offset=oQK)
        x_t = [AA(f"xt{i}", [128, H], F32, offset=oQK + i * 4096) for i in range(2)]
        v_t = AA("vt", [128, EC * S], BF16, offset=oV)
        x1_b = AA("x1b", [128, TC * H], F32, offset=oV)
        base_t = AA("baset", [128, S], F32, offset=oU)
        u_t = AA("ut", [128, EC * S], BF16, offset=oU)
        th_s = [AA(f"th{i}", [128, H], F32, offset=oU + i * 4096) for i in range(2)]
        tc_s = [AA(f"tcs{i}", [128, H], F32, offset=oU + 8192 + i * 4096) for i in range(2)]
        y8_s = [AA(f"y8{i}", [128, H], FP8, offset=oU + 16384 + i * 1024) for i in range(2)]
        y16_s = [AA(f"y16{i}", [128, H], BF16, offset=oU + 18432 + i * 2048) for i in range(2)]
        out_t = [AA(f"ot{i}", [128, H], F32, offset=oU + 22528 + i * 4096) for i in range(2)]
        lnT1 = AA("lnT1", [128, H], BF16, offset=oU + 30720)
        y_f = AA("yf", [128, KC * S], BF16, offset=oYF)
        wsl32 = [AA(f"wsl32_{i}", [128, H], F32, offset=oYF + i * 4096)
                 for i in range(2)]
        wsl16 = [AA(f"wsl16_{i}", [128, H], BF16, offset=oYF + 8192 + i * 2048)
                 for i in range(2)]
        rowtmp = AA("rowtmp", [1, E], F32, offset=oA)
        rowtmp2 = AA("rowtmp2", [1, H], F32, offset=oA + 8192)
        rowtmp3 = AA("rowtmp3", [1, H], F32, offset=oA + 12288)

        lnT_scr = [lnT0, lnT1]
        t_relu = [lnT0[:, 512:512 + NT], lnT0[:, 0:NT]]


        def x1_tile(t):
            return x1_b[:, t * H:(t + 1) * H]

        pw = [En(nc.psum_tensor(f"pw{i}", [128, 1024], F32)) for i in range(2)]
        pn = [En(nc.psum_tensor(f"pn{i}", [128, 512], F32)) for i in range(2)]
        p_mm = [pw[0][:, 0:512], pw[0][:, 512:1024]]
        PB = [pw[0][:, 0:512], pw[0][:, 512:1024],
              pw[1][:, 0:512], pw[1][:, 512:1024], pn[0][:], pn[1][:]]
        rot = [0]
        p_tp = En(nc.psum_tensor("ptp", [128, 1024], BF16))
        p_tp2 = En(nc.psum_tensor("ptp2", [128, 1024], BF16))

        sems = {n: En(nc.semaphore(n)) for n in (
            "lds", "lx0", "lx1", "w0", "w1", "o0", "o1",
            "p0", "p1", "q0", "q1", "ga0", "ga1", "pe", "v", "s", "g")}

        Z = ("pe", 0)
        st = {
            "pb": [Z] * 6, "ptp": Z, "ptp2": Z, "lnTh": [Z, Z],
            "xt": [Z, Z], "lnT": [Z, Z], "trelu": [Z, Z],
            "wbuf": [Z, Z], "wbase": Z, "outt": [Z, Z],
            "th": [Z, Z], "tcs": [Z, Z], "y8": [Z, Z], "y16": [Z, Z],
        }

        def dma(eng, dst, src, sem, waits=(), self_wait=True):
            w = list(waits)
            if self_wait:
                w.append((sem, P.val(sem)))
            return P.op(eng, lambda e, d=dst, s=src: e.dma_start(out=d, in_=s),
                        waits=w, inc=sem, inc_amt=16)

        dma("sync", ident[:], ident_d.ap(), "lds", self_wait=False)
        dma("sync", bu_cols[:], bu_d.ap(), "lds", self_wait=False)
        dma("sync", bbase_col[:], bbase_d.ap(), "lds", self_wait=False)
        dma("sync", gb_cols[:], gb_d.ap(), "lds", self_wait=False)
        dma("sync", rowtmp[:], bv_d.ap(), "lds", self_wait=False)
        dma("sync", rowtmp2[:], bout_d.ap(), "lds", self_wait=False)
        dma("sync", rowtmp3[:], bffn_d.ap(), "lds", self_wait=False)
        dma("sync", s_t[:], sin_d.ap(), "lds", self_wait=False)
        dma("sync", w_t[:], wsc_d.ap(), "lds", self_wait=False)
        ld_small = P.val("lds")

        P.op("vector", lambda e: e.memset(ones_bf[:], 1.0))
        P.op("vector", lambda e: e.memset(ones32[:], 1.0))
        P.op("vector", lambda e: e.tensor_copy(bv_bf[:], rowtmp[:]),
             waits=[("lds", ld_small)])
        P.op("vector", lambda e: e.tensor_scalar_mul(
            gb_cols[:, 0:2], gb_cols[:, 0:2], 1.0 / S))
        P.op("vector", lambda e: e.tensor_tensor(s2_t[:], s_t[:], w_t[:], op=ALU.mult))
        P.op("vector", lambda e: e.tensor_tensor(sc2_t[:], s_t[:], s_t[:], op=ALU.mult))
        P.op("vector", lambda e: e.reciprocal(i2s_t[:], s_t[:]))
        P.op("vector", lambda e: e.drain())
        vv0 = P.op("vector", lambda e: e.tensor_scalar_add(sc2_t[:], sc2_t[:], -0.0225),
                   inc="v")
        sv0 = P.op("scalar", lambda e: e.activation(mask_t[:], sc2_t[:], AF.Sign),
                   waits=[("v", vv0)], inc="s")
        P.op("vector", lambda e: e.tensor_scalar(c_t[:], mask_t[:], 0.25, 0.75,
                                                 op0=ALU.mult, op1=ALU.add),
             waits=[("s", sv0)])
        P.op("vector", lambda e: e.tensor_scalar(rci_t[:], mask_t[:], -0.5, 1.5,
                                                 op0=ALU.mult, op1=ALU.add))
        P.op("vector", lambda e: e.drain())
        P.op("vector", lambda e: e.tensor_tensor(sc2_t[:], s2_t[:], rci_t[:], op=ALU.mult))
        P.op("vector", lambda e: e.tensor_tensor(ci2s_t[:], i2s_t[:], c_t[:], op=ALU.mult))
        P.op("vector", lambda e: e.tensor_scalar_mul(cb_t[:], c_t[:], 448.0))
        P.op("vector", lambda e: e.drain())
        P.op("vector", lambda e: e.reciprocal(isc_t[:], sc2_t[:]))
        P.op("vector", lambda e: e.tensor_scalar_mul(ncb_t[:], cb_t[:], -1.0))
        P.op("vector", lambda e: e.drain())
        P.op("vector", lambda e: e.tensor_scalar_mul(rowtmp3[:], rowtmp3[:], isc_t[:]))
        P.op("vector", lambda e: e.drain())
        v_pre = P.op("vector", lambda e: e.tensor_copy(bffn_bf[:], rowtmp3[:]), inc="v")
        P.op("tensor", lambda e: e.matmul(
            PB[0][:, 0:512], ones32[:], rowtmp2[0:1, 0:512], start=True, stop=True),
            waits=[("v", v_pre), ("lds", ld_small)])
        P.op("tensor", lambda e: e.matmul(
            PB[1][:, 0:512], ones32[:], rowtmp2[0:1, 512:1024], start=True, stop=True))
        P.op("tensor", lambda e: e.matmul(
            pn[0][:, 0:1], ones32[:], sc2_t[:], start=True, stop=True))
        P.op("tensor", lambda e: e.matmul(
            pn[0][:, 1:2], ones32[:], ci2s_t[:], start=True, stop=True))
        P.op("tensor", lambda e: e.matmul(
            pn[0][:, 2:3], ones32[:], cb_t[:], start=True, stop=True))
        pe_bc = P.op("tensor", lambda e: e.matmul(
            pn[0][:, 3:4], ones32[:], ncb_t[:], start=True, stop=True), inc="pe")
        P.op("vector", lambda e: e.tensor_copy(bout_bc[:, 0:512], PB[0][:, 0:512]),
             waits=[("pe", pe_bc)])
        P.op("vector", lambda e: e.tensor_copy(bout_bc[:, 512:1024], PB[1][:, 0:512]))
        P.op("vector", lambda e: e.tensor_copy(scomb_bc[:], pn[0][:, 0:1]))
        P.op("vector", lambda e: e.tensor_copy(inv2s_bc[:], pn[0][:, 1:2]))
        P.op("vector", lambda e: e.tensor_copy(cb_bc[:], pn[0][:, 2:3]))
        v_bc = P.op("vector", lambda e: e.tensor_copy(ncb_bc[:], pn[0][:, 3:4]),
                    inc="v")
        st["pb"][0] = st["pb"][1] = ("v", v_bc)
        st["pb"][4] = ("v", v_bc)

        pre_v = P.val("v")
        scr_ready = {}
        scr_q = [0]

        def scr_store(sc_idx, src_ap, cast_v):
            qi = scr_q[0] % 2
            scr_q[0] += 1
            v = P.op("scalar", lambda e, d=sc_idx, src=src_ap:
                     e.dma_start(out=sc_d.ap()[d][:, 0:src.shape[-1]], in_=src),
                     waits=[("v", cast_v), ("q%d" % qi, P.val("q%d" % qi))],
                     inc="q%d" % qi, inc_amt=16)
            scr_ready[sc_idx] = [("q%d" % qi, v)]

        rA_free = [("v", pre_v), ("pe", pe_bc)]
        rV_free = [("v", pre_v)]
        rU_free = [("v", pre_v)]
        qk_gate = [("v", pre_v)]

        QV, QU, QB, QF = list(range(4)), list(range(4, 8)), 8, [13, 14]

        stg_free = [("v", 0), ("v", 0)]

        def load_panel(s, dst_fn, sc_idx, cols, f32_srcs, wsem, free_w, gate):
            fw = list(free_w) if isinstance(free_w, list) else [free_w]
            if s > 0:
                g = scr_ready[sc_idx] if s == 1 else []
                v = dma("sync", dst_fn(0, cols), sc_d.ap()[sc_idx][:, 0:cols],
                        wsem, waits=fw + g)
                return [(wsem, v)]
            cv = 0
            for i, (src, col, w) in enumerate(f32_srcs):
                sl = i % 2
                ldv = dma("sync", wstage[sl][:, 0:w], src, "w%d" % sl,
                          waits=[stg_free[sl]] + (fw if i == 0 else []))
                cv = P.op("vector", lambda e, sl=sl, col=col, w=w:
                          e.tensor_copy(dst_fn(col, w), wstage[sl][:, 0:w]),
                          waits=[("w%d" % sl, ldv)] + (fw if i == 0 else []),
                          inc="v")
                stg_free[sl] = ("v", cv)
            scr_store(sc_idx, dst_fn(0, cols), cv)
            return [("v", cv)]

        wpre_q = [0]
        wc8_free = [("v", 0)]

        def emit_wout_precast_piece():
            i, k = wpre_q[0] // KC, wpre_q[0] % KC
            wpre_q[0] += 1
            hh, rr = i // 2, i % 2
            row = 9 + i
            sl = wpre_q[0] % 2
            g = "ga%d" % sl
            ldv = P.op("scalar", lambda e, rr=rr, k=k, hh=hh, sl=sl: e.dma_start(
                out=wstage[sl][:],
                in_=wout_d.ap()[(rr * 8 + k) * 128:(rr * 8 + k + 1) * 128,
                                hh * 512:(hh + 1) * 512]),
                waits=[stg_free[sl], (g, P.val(g))], inc=g, inc_amt=16)
            cv = P.op("vector", lambda e, sl=sl: e.tensor_copy(
                wc8s[:], wstage[sl][:]),
                waits=[(g, ldv), wc8_free[0]], inc="v")
            stg_free[sl] = ("v", cv)
            stv = P.op("scalar", lambda e, row=row, k=k, sl=sl: e.dma_start(
                out=sc_d.ap()[row][:, k * 512:(k + 1) * 512],
                in_=wc8s[:]),
                waits=[("v", cv), ("q%d" % sl, P.val("q%d" % sl))],
                inc="q%d" % sl, inc_amt=16)
            wc8_free[0] = ("q%d" % sl, stv)
            if k == KC - 1:
                scr_ready[row] = [("q0", P.val("q0")),
                                  ("q1", P.val("q1"))]
            state["wpre_done"] = [("q0", P.val("q0")), ("q1", P.val("q1"))]

        state = {
            "p3_pe": 0, "p4_pe": 0, "ffn_pe": 0,
            "ln_evac": [("s", 0)] * TC,
            "qk_ready": 0, "v_ready": 0, "u_ready": 0,
            "sq_ready": [0, 0],
        }

        tp3d = p_tp[:].rearrange("p (c t) -> p c t", c=KC)
        tp2_3d = p_tp2[:].rearrange("p (c t) -> p c t", c=KC)

        state["stats1"] = 0

        def emit_P1A_tile(s, t, xgate=()):
            b = t % 2
            ldv = dma("sync", x_t[b][:], x_d.ap()[s, t * 128:(t + 1) * 128, :],
                      "lx%d" % b, waits=[st["xt"][b], ("pe", state["p3_pe"])] +
                      list(xgate))
            P.op("vector", lambda e, t=t, b=b: e.bn_stats(
                stbn[:, t * 12:t * 12 + 6], x_t[b][:, 0:512]),
                waits=[("lx%d" % b, ldv)])
            vv = P.op("vector", lambda e, t=t, b=b: e.bn_stats(
                stbn[:, t * 12 + 6:t * 12 + 12], x_t[b][:, 512:1024]), inc="v")
            st["xt"][b] = ("v", vv)

        pend = {"ln2_acc": 0, "tail_rc": None}

        def emit_P1A_tail(s, defer=False):
            P.op("vector", lambda e: e.drain())
            for t in range(TC):
                P.op("vector", lambda e, t=t: e.bn_aggr(
                    stmv[:, 2 * t:2 * t + 2], stbn[:, 12 * t:12 * t + 12]))
            P.op("vector", lambda e: e.drain())
            P.op("vector", lambda e: e.tensor_scalar_add(
                ve_all[:], stmv[:].rearrange("p (t c) -> p t c", c=2)[:, :, 1], 1e-5))
            P.op("vector", lambda e: e.drain())
            vv = P.op("vector", lambda e: e.reciprocal(rc_all[:], ve_all[:]), inc="v")
            if defer:
                pend["tail_rc"] = vv
                return
            sv = P.op("scalar", lambda e: e.activation(rs_all[:], rc_all[:], AF.Sqrt),
                      waits=[("v", vv)], inc="s")
            state["stats1"] = P.op("vector", lambda e: e.scalar_tensor_tensor(
                nb_all[:], stmv[:].rearrange("p (t c) -> p t c", c=2)[:, :, 0],
                -1.0, rs_all[:], op0=ALU.mult, op1=ALU.mult),
                waits=[("s", sv)], inc="v")

        ln3d = ln[:].rearrange("p (c t) -> p c t", c=KC)

        p1b_sv = {}

        def emit_P1B_norm(s, t):
            b = t % 2
            hw = [st["lnTh"][0], st["lnTh"][1]] if t % 2 == 0 else []
            if s == 0:
                sv = P.op("scalar", lambda e, t=t: e.activation(
                    lnT_scr[t % 2][:], x1_tile(t), AF.Identity,
                    scale=rs_all[:, t:t + 1], bias=nb_all[:, t:t + 1]),
                    waits=[("v", state["stats1"]), st["lnT"][t % 2]] + hw,
                    inc="s")
                p1b_sv[t] = sv
                return
            ldv = dma("sync", x_t[b][:], x_d.ap()[s, t * 128:(t + 1) * 128, :],
                      "lx%d" % b, waits=[st["xt"][b]])
            sv = P.op("scalar", lambda e, t=t, b=b: e.activation(
                lnT_scr[t % 2][:], x_t[b][:], AF.Identity,
                scale=rs_all[:, t:t + 1], bias=nb_all[:, t:t + 1]),
                waits=[("lx%d" % b, ldv), ("v", state["stats1"]),
                       st["lnT"][t % 2]] + hw,
                inc="s")
            st["xt"][b] = ("s", sv)
            p1b_sv[t] = sv

        def emit_P1B_tp(s, t, rA, pre=False):
            sv = p1b_sv[t]
            odd = pre and t % 2 == 1
            bank, b3d, key = (p_tp, tp3d, "ptp") if odd else \
                             (p_tp2, tp2_3d, "ptp2")
            pv = None
            for hc in range(KC):
                w = [("s", sv), st[key]] if hc == 0 else []
                pv = P.op("tensor", lambda e, hc=hc, t=t, bank=bank: e.transpose(
                    bank[:, hc * 128:(hc + 1) * 128],
                    lnT_scr[t % 2][:, hc * 128:(hc + 1) * 128], ident[:]),
                    waits=w, inc="pe" if hc == KC - 1 else None)
            st["lnT"][t % 2] = ("pe", pv)
            if t % 2 == 0:
                st["lnTh"][0] = st["lnTh"][1] = ("pe", pv)
            if pre:
                ev = P.op("vector", lambda e, t=t, b3d=b3d: e.tensor_copy(
                    ln3d[:, :, t * 128:(t + 1) * 128], b3d),
                    waits=[("pe", pv)] + (rA if t == 0 else []), inc="v")
                st[key] = ("v", ev)
                state["ln_evac"][t] = ("v", ev)
            else:
                ev = P.op("scalar", lambda e, t=t: e.activation(
                    ln3d[:, :, t * 128:(t + 1) * 128], tp2_3d, AF.Copy),
                    waits=[("pe", pv)] + (rA if t == 0 else []), inc="s")
                st[key] = ("s", ev)
                state["ln_evac"][t] = ("s", ev)

        def emit_P1B_tile(s, t, rA, pre=False):
            emit_P1B_norm(s, t)
            emit_P1B_tp(s, t, rA, pre=pre)

        stats2_q = [0] * ((TC + 3) // 4)

        def emit_stats_finish(s):
            w0 = [("s", pend["ln2_acc"])]
            P.op("vector", lambda e: e.tensor_scalar(
                mean_all[:], ssum[:], 1.0 / H, 0.0,
                op0=ALU.mult, op1=ALU.add), waits=w0)
            P.op("vector", lambda e: e.drain())
            P.op("vector", lambda e: e.tensor_tensor(
                rc2_all[:], mean_all[:], mean_all[:], op=ALU.mult))
            P.op("vector", lambda e: e.tensor_scalar(
                rs2s_all[:], ssq[:], 1.0 / H, 1e-5,
                op0=ALU.mult, op1=ALU.add))
            P.op("vector", lambda e: e.drain())
            P.op("vector", lambda e: e.tensor_tensor(
                ve2_all[:], rs2s_all[:], rc2_all[:], op=ALU.subtract))
            P.op("vector", lambda e: e.drain())
            rcv = P.op("vector", lambda e: e.reciprocal(
                rc2_all[:], ve2_all[:]), inc="v")
            sv = P.op("scalar", lambda e: e.activation(
                rs2_all[:, 0:TC], rc2_all[:, 0:TC], AF.Sqrt),
                waits=[("v", rcv)], inc="s")
            sv2 = None
            if pend["tail_rc"] is not None:
                sv2 = P.op("scalar", lambda e: e.activation(
                    rs_all[:], rc_all[:], AF.Sqrt),
                    waits=[("v", pend["tail_rc"])], inc="s")
            P.op("vector", lambda e: e.tensor_scalar(
                rs2s_all[:, 0:TC], rs2_all[:, 0:TC], inv2s_bc[:], 0.0,
                op0=ALU.mult, op1=ALU.add), waits=[("s", sv)])
            P.op("vector", lambda e: e.drain())
            qv = P.op("vector", lambda e: e.scalar_tensor_tensor(
                nb2_all[:, 0:TC], mean_all[:],
                -1.0, rs2s_all[:, 0:TC], op0=ALU.mult, op1=ALU.mult), inc="v")
            for q in range(len(stats2_q)):
                stats2_q[q] = qv
            if sv2 is not None:
                state["stats1"] = P.op("vector", lambda e: e.scalar_tensor_tensor(
                    nb_all[:], stmv[:].rearrange("p (t c) -> p t c", c=2)[:, :, 0],
                    -1.0, rs_all[:], op0=ALU.mult, op1=ALU.mult),
                    waits=[("s", sv2)], inc="v")
                pend["tail_rc"] = None

        def emit_P1(s, rA, xgate=()):
            if s == 0:
                for t in range(TC):
                    dma("sync", x1_tile(t),
                        x_d.ap()[0, t * 128:(t + 1) * 128, :],
                        "lx0", self_wait=False)
                allx = P.val("lx0")
                for t in range(TC):
                    P.op("vector", lambda e, t=t: e.bn_stats(
                        stbn[:, t * 12:t * 12 + 6], x1_tile(t)[:, 0:512]),
                        waits=[("lx0", allx)])
                    P.op("vector", lambda e, t=t: e.bn_stats(
                        stbn[:, t * 12 + 6:t * 12 + 12],
                        x1_tile(t)[:, 512:1024]))
            else:
                for t in range(TC):
                    emit_P1A_tile(s, t, xgate)
            emit_P1A_tail(s)
            for t in range(TC):
                emit_P1B_tile(s, t, rA, pre=True)

        def make_scores(j, sqi):
            ops = []
            for mc in range(TC):
                def emit(mc=mc, j=j, sqi=sqi):
                    b = mc % 2
                    pv = P.op("tensor", lambda e, mc=mc, j=j, b=b: e.matmul(
                        p_mm[b][:, 0:NT],
                        qk[:, S + mc * 128:S + (mc + 1) * 128],
                        qk[:, j * NT:(j + 1) * NT],
                        start=True, stop=True),
                        waits=[("s", state["qk_ready"]), st["pb"][b]],
                        inc="pe")
                    sv = P.op("scalar", lambda e, mc=mc, b=b: e.activation(
                        t_relu[b], p_mm[b][:, 0:NT], AF.Relu),
                        waits=[("pe", pv), st["trelu"][b], st["lnTh"][b]],
                        inc="s")
                    st["pb"][b] = ("s", sv)
                    vv = P.op("vector", lambda e, mc=mc, sqi=sqi, b=b: e.tensor_tensor(
                        sq_b[sqi][:, mc * NT:(mc + 1) * NT], t_relu[b],
                        t_relu[b], op=ALU.mult),
                        waits=[("s", sv)], inc="v")
                    st["trelu"][b] = ("v", vv)
                    st["lnTh"][b] = ("v", vv)
                    state["sq_ready"][sqi] = vv
                ops.append(emit)
            return ops

        pending_scores = []

        def drip_scores(n=1):
            for _ in range(n):
                if pending_scores:
                    pending_scores.pop(0)()

        emit_P1(0, rA_free, xgate=[("v", pre_v)])
        for s in range(seqs):
            ln_evac = list(state["ln_evac"])
            sq0_dead_pe = 0

            wvbw = state.pop("wvbw_next", None)
            if wvbw is None:
                wvbw = load_panel(
                    s, lambda col, w: wbase[:, col:col + w], QB, KC * 128,
                    [(wuv_d.ap()[k * 128:(k + 1) * 128, 2 * E:2 * E + SD],
                      k * 128, SD) for k in range(KC)],
                    "w0", st["wbase"], [])
            for bt in range(NBT):
                pi = rot[0] % 4
                rot[0] += 1
                tb0 = (bt * BT) // 128
                for k in range(KC):
                    w = wvbw + list(ln_evac[tb0:tb0 + BT // 128]) + \
                        [st["pb"][pi]] if k == 0 else []
                    P.op("tensor", lambda e, k=k, bt=bt, pi=pi: e.matmul(
                        PB[pi][:, 0:BT],
                        wbase[:, k * 128:(k + 1) * 128],
                        ln[:, k * S + bt * BT:k * S + (bt + 1) * BT],
                        start=(k == 0), stop=(k == KC - 1)),
                        waits=w, inc="pe" if k == KC - 1 else None)
                pv = P.val("pe")
                sv = P.op("scalar", lambda e, bt=bt, pi=pi: e.activation(
                    base_t[:, bt * BT:(bt + 1) * BT], PB[pi][:, 0:BT], AF.Silu,
                    bias=bbase_col[:]),
                    waits=[("pe", pv)] + (rU_free if bt == 0 else []), inc="s")
                st["pb"][pi] = ("s", sv)
            base_pe = P.val("pe")
            st["wbase"] = [("pe", base_pe)] + (scr_ready[QB] if s == 0 else [])
            P.op("scalar", lambda e: e.drain())
            P.op("scalar", lambda e: e.activation(
                qk[:, 0:S], base_t[:], AF.Identity,
                scale=gb_cols[:, 0:1], bias=gb_cols[:, 1:2]),
                waits=qk_gate)
            state["qk_ready"] = P.op("scalar", lambda e: e.activation(
                qk[:, S:2 * S], base_t[:], AF.Identity,
                scale=gb_cols[:, 2:3], bias=gb_cols[:, 3:4]), inc="s")
            P.op("scalar", lambda e: e.drain())

            for qi in range(4):
                wsl = qi % 2
                wvw = load_panel(
                    s, lambda col, w, wsl=wsl: wbuf[wsl][:, col:col + w],
                    QV[qi], 4096,
                    [(wuv_d.ap()[k * 128:(k + 1) * 128,
                                 E + qi * 512:E + (qi + 1) * 512], k * 512, 512)
                     for k in range(KC)],
                    "w%d" % wsl, st["wbuf"][wsl], [])
                last_pe = None
                for t in range(TC):
                    pi = rot[0] % 4
                    rot[0] += 1
                    for k in range(KC):
                        w = wvw + [ln_evac[t], st["pb"][pi]] \
                            if k == 0 else []
                        P.op("tensor", lambda e, k=k, t=t, pi=pi, wsl=wsl: e.matmul(
                            PB[pi][:, 0:512],
                            ln[:, k * S + t * 128:k * S + (t + 1) * 128],
                            wbuf[wsl][:, k * 512:(k + 1) * 512],
                            start=(k == 0), stop=False), waits=w)
                    pv = P.op("tensor", lambda e, q=qi, pi=pi: e.matmul(
                        PB[pi][:, 0:512], ones_bf[:],
                        bv_bf[0:1, q * 512:(q + 1) * 512],
                        start=False, stop=True), inc="pe")
                    last_pe = pv
                    sv = P.op("scalar", lambda e, t=t, q=qi, pi=pi: e.activation(
                        v_t[:, t * E + q * 512:t * E + (q + 1) * 512],
                        PB[pi][:, 0:512], AF.Silu),
                        waits=[("pe", pv)] + (rV_free if (t == 0 and qi == 0) else []),
                        inc="s")
                    st["pb"][pi] = ("s", sv)
                st["wbuf"][wsl] = [("pe", last_pe)] + \
                    (scr_ready[QV[qi]] if s == 0 else [])
            state["v_ready"] = P.val("s")

            mm_count = 0
            for qi in range(4):
                wsl = qi % 2
                wvw = load_panel(
                    s, lambda col, w, wsl=wsl: wbuf[wsl][:, col:col + w],
                    QU[qi], 4096,
                    [(wuv_d.ap()[k * 128:(k + 1) * 128,
                                 qi * 512:(qi + 1) * 512], k * 512, 512)
                     for k in range(KC)],
                    "w%d" % wsl, st["wbuf"][wsl], [])
                last_pe = None
                for es in range(4):
                    ec = qi * 4 + es
                    for ntj in range(NNT):
                        pi = rot[0] % 4
                        rot[0] += 1
                        t_lo, t_hi = (ntj * NT) // 128, ((ntj + 1) * NT - 1) // 128
                        for k in range(KC):
                            w = wvw + list(ln_evac[t_lo:t_hi + 1]) + \
                                [st["pb"][pi]] if k == 0 else []
                            P.op("tensor", lambda e, k=k, es=es, ntj=ntj, pi=pi, wsl=wsl: e.matmul(
                                PB[pi][:, 0:NT],
                                wbuf[wsl][:, k * 512 + es * 128:k * 512 + (es + 1) * 128],
                                ln[:, k * S + ntj * NT:k * S + (ntj + 1) * NT],
                                start=(k == 0), stop=(k == KC - 1)),
                                waits=w, inc="pe" if k == KC - 1 else None)
                        pv = P.val("pe")
                        last_pe = pv
                        sv = P.op("scalar", lambda e, ec=ec, ntj=ntj, pi=pi: e.activation(
                            u_t[:, ec * S + ntj * NT:ec * S + (ntj + 1) * NT],
                            PB[pi][:, 0:NT], AF.Silu, bias=bu_cols[:, ec:ec + 1]),
                            waits=[("pe", pv)], inc="s")
                        st["pb"][pi] = ("s", sv)
                        mm_count += 1
                st["wbuf"][wsl] = [("pe", last_pe)] + \
                    (scr_ready[QU[qi]] if s == 0 else [])
            state["u_ready"] = P.val("s")

            wf0 = load_panel(
                s, lambda col, w: wbuf[0][:, col:col + w], QF[0], 4096,
                [(wffnT_d.ap()[k * 128:(k + 1) * 128, h * 512:(h + 1) * 512],
                  (k % 4) * 1024 + h * 512, 512)
                 for k in range(4) for h in range(2)],
                "w0", st["wbuf"][0], [])
            wf1 = load_panel(
                s, lambda col, w: wbuf[1][:, col:col + w], QF[1], 4096,
                [(wffnT_d.ap()[k * 128:(k + 1) * 128, h * 512:(h + 1) * 512],
                  ((k - 4) % 4) * 1024 + h * 512, 512)
                 for k in range(4, 8) for h in range(2)],
                "w1", st["wbuf"][1], [])

            p1a_q = list(range(TC)) if s + 1 < seqs else []
            p1a_tail_due = s + 1 < seqs

            def drip_p1a(gate=()):
                if p1a_q:
                    emit_P1A_tile(s + 1, p1a_q.pop(0), xgate=gate)

            h_done_v = 0
            for j in range(NNT):
                sqi = j % 2
                own_scores = make_scores(j, sqi) if j == 0 else []
                sq_mc_v = [0] * TC
                if j == 0:
                    for k in range(min(3, TC)):
                        own_scores[k]()
                        sq_mc_v[k] = state["sq_ready"][sqi]
                if j + 1 < NNT:
                    pending_scores = make_scores(j + 1, (j + 1) % 2)
                else:
                    pending_scores = []
                for eq in range(4):
                    es_pe = [0] * 4
                    if j == 0 and eq == 0:
                        for mc in range(TC):
                            for idx in (2 * mc + 3, 2 * mc + 4):
                                if idx < TC:
                                    own_scores[idx]()
                                    sq_mc_v[idx] = state["sq_ready"][sqi]
                            for es in range(4):
                                w = [("v", sq_mc_v[mc])]
                                if mc == 0:
                                    w += [("s", state["v_ready"]), st["pb"][2 + es]]
                                P.op("tensor", lambda e, eq=eq, mc=mc, es=es, sqi=sqi: e.matmul(
                                    PB[2 + es][:, 0:NT],
                                    v_t[:, mc * E + eq * 512 + es * 128:
                                        mc * E + eq * 512 + (es + 1) * 128],
                                    sq_b[sqi][:, mc * NT:(mc + 1) * NT],
                                    start=(mc == 0), stop=(mc == TC - 1)),
                                    waits=w,
                                    inc="pe" if mc == TC - 1 else None)
                                if mc == TC - 1:
                                    es_pe[es] = P.val("pe")
                    else:
                        for es in range(4):
                            for mc in range(TC):
                                w = []
                                if mc == 0:
                                    w = [("v", state["sq_ready"][sqi]),
                                         ("s", state["v_ready"]), st["pb"][2 + es]]
                                P.op("tensor", lambda e, eq=eq, mc=mc, es=es, sqi=sqi: e.matmul(
                                    PB[2 + es][:, 0:NT],
                                    v_t[:, mc * E + eq * 512 + es * 128:
                                        mc * E + eq * 512 + (es + 1) * 128],
                                    sq_b[sqi][:, mc * NT:(mc + 1) * NT],
                                    start=(mc == 0), stop=(mc == TC - 1)),
                                    waits=w,
                                    inc="pe" if mc == TC - 1 else None)
                            es_pe[es] = P.val("pe")
                            drip_scores(1)
                            ec = eq * 4 + es
                            vv = P.op("vector", lambda e, ec=ec, es=es, j=j: e.tensor_tensor(
                                u_t[:, ec * S + j * NT:ec * S + (j + 1) * NT],
                                PB[2 + es][:, 0:NT],
                                u_t[:, ec * S + j * NT:ec * S + (j + 1) * NT],
                                op=ALU.mult),
                                waits=[("pe", es_pe[es]), ("s", state["u_ready"])],
                                inc="v")
                            st["pb"][2 + es] = ("v", vv)
                            h_done_v = vv
                            drip_scores(1)
                            if s == 0 and wpre_q[0] < 4 * KC:
                                emit_wout_precast_piece()
                    if j == 0 and eq == 0:
                        for es in range(4):
                            ec = eq * 4 + es
                            vv = P.op("vector", lambda e, ec=ec, es=es, j=j: e.tensor_tensor(
                                u_t[:, ec * S + j * NT:ec * S + (j + 1) * NT],
                                PB[2 + es][:, 0:NT],
                                u_t[:, ec * S + j * NT:ec * S + (j + 1) * NT],
                                op=ALU.mult),
                                waits=[("pe", es_pe[es]), ("s", state["u_ready"])], inc="v")
                            st["pb"][2 + es] = ("v", vv)
                            h_done_v = vv
                    if j == NNT - 1 and j > 0:
                        drip_p1a(gate=[("pe", P.val("pe"))])
                drip_scores(TC)
                if sqi == 0:
                    sq0_dead_pe = P.val("pe")
            h_ready = h_done_v
            if s == 0:
                while wpre_q[0] < 4 * KC:
                    emit_wout_precast_piece()
            p3_pe = P.val("pe")
            state["p3_pe"] = p3_pe

            wo_w = [None, None]
            for i in range(4):
                gate = sq0_dead_pe if i < 2 and NNT >= 2 else p3_pe
                hh, rr = i // 2, i % 2
                load_panel(
                    max(s, 1), lambda col, w, i=i: wout_big[:, i * 4096 + col:
                                                            i * 4096 + col + w],
                    9 + i, 4096, [],
                    "w%d" % (i % 2), ("pe", gate), [])
                if i == 1:
                    wo_w[0] = [("w0", P.val("w0")), ("w1", P.val("w1")),
                               ("v", P.val("v"))]
            wo_w[1] = [("w0", P.val("w0")), ("w1", P.val("w1")),
                       ("v", P.val("v"))]
            x1_pre = {}

            def emit_x1pre(t):
                b = t % 2
                ldv = dma("sync", x_t[b][:], x_d.ap()[s, t * 128:(t + 1) * 128, :],
                          "lx%d" % b, waits=[st["xt"][b], ("pe", p3_pe)])
                vv = P.op("vector", lambda e, t=t, b=b: e.scalar_tensor_tensor(
                    x1_tile(t), x_t[b][:], 1.0, bout_bc[:],
                    op0=ALU.mult, op1=ALU.add),
                    waits=[("lx%d" % b, ldv), ("pe", p3_pe)], inc="v")
                st["xt"][b] = ("v", vv)
                x1_pre[t] = ("v", vv)

            for t in range(min(2, TC)):
                emit_x1pre(t)
            for t in range(TC):
                wi = t % 2
                x1ld = x1_pre[t]
                for hh in range(2):
                    for ec in range(EC):
                        w = []
                        if ec == 0:
                            w = wo_w[hh] + [("v", h_ready),
                                            st["pb"][2 * wi], st["pb"][2 * wi + 1]]
                        P.op("tensor", lambda e, ec=ec, t=t, hh=hh, wi=wi: e.matmul(
                            pw[wi][:, hh * 512:(hh + 1) * 512],
                            u_t[:, ec * S + t * 128:ec * S + (t + 1) * 128],
                            wout_big[:, hh * 8192 + (ec // 8) * 4096 +
                                     (ec % 8) * 512:
                                     hh * 8192 + (ec // 8) * 4096 + (ec % 8) * 512 + 512],
                            start=(ec == 0), stop=(ec == EC - 1)),
                            waits=w, inc="pe" if (ec == EC - 1 and hh == 1) else None)
                pv = P.val("pe")
                vv = P.op("vector", lambda e, t=t, wi=wi: e.tensor_tensor(
                    x1_tile(t), pw[wi][:, 0:1024], x1_tile(t), op=ALU.add),
                    waits=[("pe", pv), x1ld], inc="v")
                st["pb"][2 * wi] = st["pb"][2 * wi + 1] = ("v", vv)
                x1_done = P.val("v")
                s1 = P.op("scalar", lambda e, t=t: e.activation(
                    lnT0[:], x1_tile(t), AF.Square,
                    accum_out=ssq[:, t:t + 1]),
                    waits=[("v", x1_done), ("s", pend["ln2_acc"])] +
                          ([st["lnTh"][0], st["lnTh"][1], st["lnT"][0]]
                           if t == 0 else []), inc="s")
                s2 = P.op("scalar", lambda e, t=t: e.activation(
                    lnT0[:], x1_tile(t), AF.Identity,
                    accum_out=ssum[:, t:t + 1]),
                    waits=[("s", s1)], inc="s")
                st["lnTh"][0] = st["lnTh"][1] = ("s", s2)
                pend["ln2_acc"] = s2
                if t + 2 < TC:
                    emit_x1pre(t + 2)
                drip_p1a()
                if p1a_tail_due and not p1a_q:
                    emit_P1A_tail(s + 1, defer=True)
                    p1a_tail_due = False
            rA_next = [("pe", P.val("pe"))]
            p4_pe = P.val("pe")
            state["p4_pe"] = p4_pe
            emit_stats_finish(s)

            yf3d = y_f[:].rearrange("p (c t) -> p c t", c=KC)
            y_evac = [0] * TC
            y_evac2 = [0] * TC
            th_v = [0] * TC
            q_done = [0] * TC

            def emit_th(t):
                b = t % 2
                th_v[t] = P.op("scalar", lambda e, t=t, b=b: e.activation(
                    th_s[b][:], x1_tile(t), AF.Identity,
                    scale=rs2s_all[:, t:t + 1], bias=nb2_all[:, t:t + 1]),
                    waits=[("v", stats2_q[t // 4]), st["th"][b], ("pe", p4_pe)],
                    inc="s")

            def emit_quant(t):
                b = t % 2
                vv = P.op("vector", lambda e, b=b: e.tensor_scalar(
                    tc_s[b][:], th_s[b][:], cb_bc[:], ncb_bc[:],
                    op0=ALU.min, op1=ALU.max),
                    waits=[("s", th_v[t]), st["tcs"][b]], inc="v")
                st["th"][b] = ("v", vv)
                sv = P.op("scalar", lambda e, b=b: e.activation(
                    y8_s[b][:], tc_s[b][:], AF.Copy),
                    waits=[("v", vv), st["y8"][b]], inc="s")
                st["tcs"][b] = ("s", sv)
                vv = P.op("vector", lambda e, b=b: e.tensor_copy(
                    y16_s[b][:], y8_s[b][:]),
                    waits=[("s", sv), st["y16"][b]], inc="v")
                st["y8"][b] = ("v", vv)
                q_done[t] = vv

            pending_out = None

            def emit_out_stt(t, wi, ob, pv):
                vv = P.op("vector", lambda e, t=t, wi=wi, ob=ob: e.scalar_tensor_tensor(
                    out_t[ob][:], pw[wi][:, 0:1024], scomb_bc[:],
                    x1_tile(t), op0=ALU.mult, op1=ALU.add),
                    waits=[("pe", pv), st["outt"][ob]], inc="v")
                st["pb"][2 * wi] = st["pb"][2 * wi + 1] = ("v", vv)
                stv = dma("sync", out_d.ap()[s, t * 128:(t + 1) * 128, :], out_t[ob][:],
                          "o%d" % ob, waits=[("v", P.val("v"))])
                st["outt"][ob] = ("o%d" % ob, stv)

            emit_th(0)
            if TC > 1:
                emit_th(1)
            emit_quant(0)
            if s + 1 < seqs:
                emit_P1B_norm(s + 1, 0)
            for t in range(TC):
                b = t % 2
                pv = None
                for hc in range(KC):
                    w = [("v", q_done[t]), st["ptp"]] if hc == 0 else []
                    pv = P.op("tensor", lambda e, hc=hc, b=b: e.transpose(
                        p_tp[:, hc * 128:(hc + 1) * 128],
                        y16_s[b][:, hc * 128:(hc + 1) * 128], ident[:]),
                        waits=w, inc="pe" if hc == KC - 1 else None)
                st["y16"][b] = ("pe", pv)
                vv = P.op("vector", lambda e, t=t: e.tensor_copy(
                    yf3d[:, 0:KC // 2, t * 128:(t + 1) * 128],
                    tp3d[:, 0:KC // 2, :]),
                    waits=[("pe", pv)] +
                          (state.get("wpre_done", [])
                           if s == 0 and t == 0 else []), inc="v")
                y_evac[t] = vv
                vv = P.op("vector", lambda e, t=t: e.tensor_copy(
                    yf3d[:, KC // 2:KC, t * 128:(t + 1) * 128],
                    tp3d[:, KC // 2:KC, :]), inc="v")
                st["ptp"] = ("v", vv)
                y_evac2[t] = vv
                if t + 1 < TC:
                    emit_quant(t + 1)
                if t == min(8, TC - 1) and s + 1 < seqs:
                    state["wvbw_next"] = load_panel(
                        s + 1, lambda col, w: wbase[:, col:col + w],
                        QB, KC * 128, [], "w0", st["wbase"], [])
                if s + 1 < seqs:
                    emit_P1B_tp(s + 1, t, rA_next)
                    if t + 1 < TC:
                        emit_P1B_norm(s + 1, t + 1)
                if t + 2 < TC:
                    emit_th(t + 2)
                if pending_out is not None:
                    emit_out_stt(*pending_out)
                ob = t % 2
                wi = t % 2
                for ot in range(2):
                    for k in range(KC):
                        w = []
                        if k == 0:
                            w = wf0 + wf1 + [("v", y_evac[t]),
                                 st["pb"][2 * wi], st["pb"][2 * wi + 1]]
                        elif k == KC // 2:
                            w = [("v", y_evac2[t])]
                        P.op("tensor", lambda e, k=k, t=t, ot=ot, wi=wi: e.matmul(
                            pw[wi][:, ot * 512:(ot + 1) * 512],
                            y_f[:, k * S + t * 128:k * S + (t + 1) * 128],
                            wbuf[k // 4][:, (k % 4) * 1024 + ot * 512:
                                         (k % 4) * 1024 + (ot + 1) * 512],
                            start=(k == 0), stop=False), waits=w)
                    P.op("tensor", lambda e, ot=ot, wi=wi: e.matmul(
                        pw[wi][:, ot * 512:(ot + 1) * 512], ones_bf[:],
                        bffn_bf[0:1, ot * 512:(ot + 1) * 512],
                        start=False, stop=True),
                        inc="pe" if ot == 1 else None)
                pending_out = (t, wi, ob, P.val("pe"))
            emit_out_stt(*pending_out)
            pending_out = None
            ffn_pe = P.val("pe")
            st["wbuf"][0] = [("pe", ffn_pe)] + (scr_ready[QF[0]] if s == 0 else [])
            st["wbuf"][1] = [("pe", ffn_pe)] + (scr_ready[QF[1]] if s == 0 else [])

            rV_free = [("v", P.val("v")), ("s", P.val("s"))]
            rU_free = [("o0", P.val("o0")), ("o1", P.val("o1")), ("v", P.val("v")),
                       ("pe", ffn_pe), ("s", P.val("s"))]
            qk_gate = []
            P.op("vector", lambda e: e.drain())
            P.op("scalar", lambda e: e.drain())

        P.op("sync", lambda e: e.nop(),
             waits=[("o0", P.val("o0")), ("o1", P.val("o1"))])

        with nc.Block() as block:
            def body(eng_name):
                def f(eng):
                    for waits, fn, inc, inc_amt in P.progs[eng_name]:
                        for sn, v in waits:
                            eng.wait_ge(sems[sn], v)
                        r = fn(eng)
                        if inc is not None:
                            r.then_inc(sems[inc], inc_amt)
                return f
            block.sync(body("sync"))
            block.scalar(body("scalar"))
            block.vector(body("vector"))
            block.tensor(body("tensor"))
            block.gpsimd(body("gpsimd"))

    return nc


def prep_aux(L0_attn_W_uv, L0_attn_b_uv, L0_attn_gamma, L0_attn_beta,
             L0_attn_W_out, L0_attn_b_out, L0_ffn_weight_fp8, L0_ffn_bias,
             L0_ffn_input_scale, L0_ffn_weight_scale):
    gb = np.stack([
        np.asarray(L0_attn_gamma)[0], np.asarray(L0_attn_beta)[0],
        np.asarray(L0_attn_gamma)[1], np.asarray(L0_attn_beta)[1],
    ], axis=1).astype(np.float32)
    b_uv = np.asarray(L0_attn_b_uv, dtype=np.float32)
    return {
        "wuv": np.ascontiguousarray(np.asarray(L0_attn_W_uv, dtype=np.float32)),
        "wout": np.ascontiguousarray(np.asarray(L0_attn_W_out, dtype=np.float32)),
        "wffnT": np.ascontiguousarray(np.asarray(L0_ffn_weight_fp8, dtype=np.float32).T),
        "ident": np.eye(128, dtype=ml_dtypes.bfloat16),
        "bu_cols": np.ascontiguousarray(b_uv[0:E].reshape(EC, 128).T),
        "bbase_col": np.ascontiguousarray(b_uv[2 * E:2 * E + SD].reshape(128, 1)),
        "gb_cols": np.ascontiguousarray(gb),
        "bv_row": np.ascontiguousarray(b_uv[E:2 * E].reshape(1, E)),
        "bout_row": np.ascontiguousarray(
            np.asarray(L0_attn_b_out, dtype=np.float32).reshape(1, H)),
        "bffn_row": np.ascontiguousarray(
            np.asarray(L0_ffn_bias, dtype=np.float32).reshape(1, H)),
        "s_in": np.asarray(L0_ffn_input_scale, dtype=np.float32).reshape(1, 1),
        "w_scale": np.asarray(L0_ffn_weight_scale, dtype=np.float32).reshape(1, 1),
    }


def kernel(L0_attn_inputs, L0_attn_W_uv, L0_attn_b_uv, L0_attn_gamma,
           L0_attn_beta, L0_attn_W_out, L0_attn_b_out, L0_ffn_weight_fp8,
           L0_ffn_bias, L0_ffn_input_scale, L0_ffn_weight_scale):
    n_cores = 8
    x = np.ascontiguousarray(np.asarray(L0_attn_inputs, dtype=np.float32))
    B, S, _ = x.shape
    seqs = B // n_cores
    aux = prep_aux(L0_attn_W_uv, L0_attn_b_uv, L0_attn_gamma, L0_attn_beta,
                   L0_attn_W_out, L0_attn_b_out, L0_ffn_weight_fp8,
                   L0_ffn_bias, L0_ffn_input_scale, L0_ffn_weight_scale)
    nc = build_program(seqs, S)
    in_maps = []
    for c in range(n_cores):
        m = {"x": x[c * seqs:(c + 1) * seqs]}
        m.update(aux)
        in_maps.append(m)
    res = run_bass_kernel_spmd(nc, in_maps, core_ids=list(range(n_cores)))
    return np.concatenate([r["out"] for r in res.results], axis=0)
